# version 02
# baseline (speedup 1.0000x reference)
import math
import os
import numpy as np
from contextlib import ExitStack

import concourse.bass as bass
import concourse.mybir as mybir
from concourse.bass_utils import run_bass_kernel_spmd

F32 = mybir.dt.float32
BF16 = mybir.dt.bfloat16
AF = mybir.ActivationFunctionType
ALU = mybir.AluOpType
AX = mybir.AxisListType

P = 128
T = 1024
LC = 8
LT = 2
D = 2048
DC = 16
FF = 8192
FC = 64
H = 16
EB = 4
NCORE = 8

INV_SCALE = 1.0 / math.sqrt(128.0)
SRMS_EXPB = 3.5 * math.log(2.0)
LN_EPS = 1e-5
SRMS_GUARD = 1e-30
INV_CLAMP = 1e12
THR_SCALE = 0.7 / float(8192 * 2048)

GSIZE = 6
FC_GROUPS = [list(range(g, min(g + GSIZE, FC))) for g in range(0, FC, GSIZE)]

B1C = 0
THRC = 64
WPC = 68
WPT = 70
OCC = 72
CSTC = 72
ONES = 80
THS = 208


class Sem:
    __slots__ = ("h", "n")

    def __init__(self, handle):
        self.h = handle
        self.n = 0

    def inc(self, instr, k=1):
        instr.then_inc(self.h, k)
        self.n += k
        return self.n


def build_full(stop_after=None, tap=None):
    nc = bass.Bass("TRN2", target_bir_lowering=False)

    d_xT = nc.dram_tensor("xT", [P, DC * T], F32, kind="ExternalInput").ap()
    d_xtm = nc.dram_tensor("xtm", [P, LC * D], F32, kind="ExternalInput").ap()
    d_wk = nc.dram_tensor("wk_r", [P, EB * DC * 512], F32, kind="ExternalInput").ap()
    d_wv = nc.dram_tensor("wv_r", [P, EB * DC * 512], F32, kind="ExternalInput").ap()
    d_wo = nc.dram_tensor("wo_r", [P, EB * DC * 512], F32, kind="ExternalInput").ap()
    d_wq = nc.dram_tensor("wq_l", [P, H * DC * 128], F32, kind="ExternalInput").ap()
    d_wu = nc.dram_tensor("wu_l", [P, H * DC * 128], F32, kind="ExternalInput").ap()
    d_w1 = nc.dram_tensor("w1_l", [P, FC * DC * 128], F32, kind="ExternalInput").ap()
    d_w2 = nc.dram_tensor("w2_r", [P, FC * D], F32, kind="ExternalInput").ap()
    d_w1s = nc.dram_tensor("w1slice", [P, 8 * 2048], F32, kind="ExternalInput").ap()
    d_w2s = nc.dram_tensor("w2slice", [P, 8 * 2048], F32, kind="ExternalInput").ap()
    d_b1 = nc.dram_tensor("b1r", [P, FC], F32, kind="ExternalInput").ap()
    d_b2 = nc.dram_tensor("b2row", [1, D], F32, kind="ExternalInput").ap()
    d_cst = nc.dram_tensor("consts", [P, 8], F32, kind="ExternalInput").ap()

    d_out = nc.dram_tensor("out", [P, LC * D], F32, kind="ExternalOutput").ap()
    d_dbg = None
    if tap is not None:
        d_dbg = nc.dram_tensor("dbg", [P, DC * T], F32, kind="ExternalOutput").ap()

    d_s = nc.dram_tensor("s_spill", [T, D], F32).ap()
    t_ccin = nc.dram_tensor("cc_in", [P + 1, D], F32)
    t_ccout = nc.dram_tensor("cc_out", [P + 1, D], F32)
    d_ccin = t_ccin.ap()
    d_ccout = t_ccout.ap()

    with ExitStack() as ctx:
        En = ctx.enter_context
        XnT = En(nc.sbuf_tensor("XnT", [P, DC * T], BF16))
        AB = En(nc.sbuf_tensor("AB", [P, DC * T], BF16))
        OUTT = En(nc.sbuf_tensor("OUTT", [P, DC * T], BF16))
        GA = En(nc.sbuf_tensor("GA", [P, DC * T], F32))
        WST = En(nc.sbuf_tensor("WST", [P, 2 * 2048], F32))
        WBF = En(nc.sbuf_tensor("WBF", [P, 3 * 2048], BF16))
        MISC = En(nc.sbuf_tensor("MISC", [P, 224], F32))
        MISCB = En(nc.sbuf_tensor("MISCB", [P, D], BF16))
        ROWS = En(nc.sbuf_tensor("ROWS", [P, 3072], F32))
        PS = [En(nc.psum_tensor(f"PS{i}", [P, 1024], F32)) for i in range(4)]

        sDMA = Sem(En(nc.semaphore("sDMA")))
        sSCD = Sem(En(nc.semaphore("sSCD")))
        sGD = Sem(En(nc.semaphore("sGD")))
        sPE = Sem(En(nc.semaphore("sPE")))
        sDVE = Sem(En(nc.semaphore("sDVE")))
        sACT = Sem(En(nc.semaphore("sACT")))
        sPOOL = Sem(En(nc.semaphore("sPOOL")))
        sCC = Sem(En(nc.semaphore("sCC")))
        sB2 = Sem(En(nc.semaphore("sB2")))
        sROW = Sem(En(nc.semaphore("sROW")))
        sTHR = Sem(En(nc.semaphore("sTHR")))
        sW0 = Sem(En(nc.semaphore("sW0")))
        sW1 = Sem(En(nc.semaphore("sW1")))
        semW = [sW0, sW1]
        sST0 = Sem(En(nc.semaphore("sST0")))
        sST1 = Sem(En(nc.semaphore("sST1")))
        semST = [sST0, sST1]
        ALLSEMS = [sDMA, sSCD, sGD, sPE, sDVE, sACT, sPOOL, sCC, sB2, sROW,
                   sTHR, sW0, sW1, sST0, sST1]

        block = En(nc.Block())

        def W(eng, sem, n):
            if n > 0:
                eng.wait_ge(sem.h, n)

        def wst(i):
            return WST[:, i * 2048:(i + 1) * 2048]

        def wbf(i):
            return WBF[:, i * 2048:(i + 1) * 2048]

        def bank(i):
            return PS[i // 2][:, (i % 2) * 512:(i % 2) * 512 + 512]

        onesrow = MISC[0:1, ONES:ONES + 128]
        onesmat = MISC[:, ONES:ONES + 128]
        eps_ap = MISC[:, CSTC + 0:CSTC + 1]
        expb_ap = MISC[:, CSTC + 1:CSTC + 2]
        guard_ap = MISC[:, CSTC + 2:CSTC + 3]

        wst_free = [(sDVE, 0), (sDVE, 0)]
        wbf_free = [(sPE, 0), (sPE, 0), (sPE, 0)]
        st8 = {"wst": 1, "wbf": 0}
        wst_ready = {}
        wbf_ready = {}

        def wload(src_ap, jid):
            i = st8["wst"]
            st8["wst"] = 1 - i

            def f(sync, i=i, src_ap=src_ap, jid=jid):
                s, n = wst_free[i]
                W(sync, s, n)
                c = semW[i].inc(sync.dma_start(out=wst(i), in_=src_ap), 16)
                wst_ready[jid] = (i, c)
            block.sync(f)

        def wcast(jid):
            j = st8["wbf"]
            st8["wbf"] = (j + 1) % 3

            def f(pool, j=j, jid=jid):
                i, c = wst_ready[jid]
                s, n = wbf_free[j]
                W(pool, s, n)
                W(pool, semW[i], c)
                cc = sPOOL.inc(pool.tensor_copy(out=wbf(j), in_=wst(i)), 1)
                wst_free[i] = (sPOOL, cc)
                wbf_ready[jid] = (j, sPOOL, cc)
            block.gpsimd(f)

        def fA_loads(sync):
            sDMA.inc(sync.dma_start(out=GA[:, 0:8192], in_=d_xT[:, 0:8192]), 16)
            sDMA.inc(sync.dma_start(out=GA[:, 8192:16384], in_=d_xT[:, 8192:16384]), 16)
            sDMA.inc(sync.dma_start(out=MISC[:, CSTC:CSTC + 8], in_=d_cst[:, :]), 16)
            sDMA.inc(sync.dma_start(out=MISC[:, B1C:B1C + FC], in_=d_b1[:, :]), 16)
        block.sync(fA_loads)
        aload = sDMA.n

        def fA_memset(vector):
            sDVE.inc(vector.memset(MISC[:, ONES:ONES + 128], 1.0), 1)
        block.vector(fA_memset)
        v_ones = sDVE.n

        sumsq_mm = []
        sq_cnt = []
        for dc in range(DC):
            def fsq(scalar, dc=dc):
                if dc == 0:
                    W(scalar, sDMA, aload)
                if dc >= 2:
                    W(scalar, sPE, sumsq_mm[dc - 2])
                c = sACT.inc(scalar.activation(
                    out=ROWS[:, (dc % 2) * 1024:(dc % 2) * 1024 + 1024],
                    in_=GA[:, dc * T:dc * T + 1024], func=AF.Square), 1)
                sq_cnt.append(c)
            block.scalar(fsq)

            def fmm(tensor, dc=dc):
                if dc == 0:
                    W(tensor, sDMA, aload)
                    W(tensor, sDVE, v_ones)
                for lt in range(LT):
                    tensor.matmul(bank(lt), onesmat,
                                  GA[:, dc * T + lt * 512: dc * T + lt * 512 + 512],
                                  start=(dc == 0), stop=(dc == DC - 1))
                W(tensor, sACT, sq_cnt[dc])
                i = None
                for lt in range(LT):
                    i = tensor.matmul(
                        bank(2 + lt), onesmat,
                        ROWS[:, (dc % 2) * 1024 + lt * 512:(dc % 2) * 1024 + lt * 512 + 512],
                        start=(dc == 0), stop=(dc == DC - 1))
                sumsq_mm.append(sPE.inc(i, 1))
            block.tensor(fmm)
        a_stats_mm = sPE.n

        MB = ROWS[:, 0:1024]
        EXB = ROWS[:, 1024:2048]
        SCB = ROWS[:, 2048:3072]

        def fA_evac(scalar):
            W(scalar, sPE, a_stats_mm)
            for lt in range(LT):
                scalar.activation(out=MB[:, lt * 512:lt * 512 + 512],
                                  in_=bank(lt), func=AF.Copy, scale=1.0 / D)
            for lt in range(LT):
                sACT.inc(scalar.activation(out=EXB[:, lt * 512:lt * 512 + 512],
                                           in_=bank(2 + lt), func=AF.Copy,
                                           scale=1.0 / D), 1)
        block.scalar(fA_evac)
        a_mrow = sACT.n

        def fA_var(vector):
            W(vector, sACT, a_mrow)
            c = sDVE.inc(vector.tensor_tensor(out=SCB, in0=MB, in1=MB,
                                              op=ALU.mult), 1)
            W(vector, sDVE, c)
            sDVE.inc(vector.tensor_tensor(out=SCB, in0=EXB, in1=SCB,
                                          op=ALU.subtract), 1)
        block.vector(fA_var)
        a_var = sDVE.n

        def fA_rstd(scalar):
            W(scalar, sDVE, a_var)
            scalar.activation(out=EXB, in_=SCB, func=AF.Ln, bias=eps_ap)
            sACT.inc(scalar.activation(out=SCB, in_=EXB, func=AF.Exp,
                                       scale=-0.5), 1)
        block.scalar(fA_rstd)
        a_rstd = sACT.n

        apply_cnt = []
        def fA_apply(vector):
            W(vector, sACT, a_rstd)
            for dc in range(DC):
                tmp = WST[:, (dc % 2) * 1024:(dc % 2) * 1024 + 1024]
                c = sDVE.inc(vector.tensor_tensor(
                    out=tmp, in0=GA[:, dc * T:dc * T + 1024],
                    in1=MB, op=ALU.subtract), 1)
                W(vector, sDVE, c)
                apply_cnt.append(sDVE.inc(vector.tensor_tensor(
                    out=XnT[:, dc * T:dc * T + 1024], in0=tmp, in1=SCB,
                    op=ALU.mult), 1))
        block.vector(fA_apply)
        a_apply = sDVE.n
        wst_free[0] = (sDVE, a_apply)
        wst_free[1] = (sDVE, 0)

        if stop_after == "A":
            return _finish(nc, block, ALLSEMS, tap, locals())

        bank_free = [(sACT, a_mrow)] * 4 + [(sDVE, 0)] * 4
        kv_evac = []
        stats_k = 0

        for eb in range(EB):
            for wname, dap in (("k", d_wk), ("v", d_wv)):
                evac_ready = {}
                last_slot = [0]
                for cg in range(4):
                    j = f"B{wname}{eb}_{cg}"
                    off = (eb * DC + cg * 4) * 512
                    wload(dap[:, off:off + 2048], j)
                    wcast(j)

                    def fchunk(tensor, eb=eb, wname=wname, cg=cg, j=j,
                               evac_ready=evac_ready):
                        if eb == 0 and wname == "k" and cg == 0:
                            W(tensor, sDVE, a_apply)
                        slot, s, c = wbf_ready[j]
                        W(tensor, s, c)
                        if cg == 0:
                            for l in range(8):
                                W(tensor, *bank_free[l])
                        for dc in range(cg * 4, cg * 4 + 4):
                            for l in range(LC):
                                i = tensor.matmul(
                                    bank(l),
                                    XnT[:, dc * T + l * 128: dc * T + l * 128 + 128],
                                    wbf(slot)[:, (dc % 4) * 512:(dc % 4) * 512 + 512],
                                    start=(dc == 0), stop=(dc == DC - 1))
                                if dc == DC - 1:
                                    evac_ready[l] = sPE.inc(i, 1)
                                elif dc % 4 == 3 and l == LC - 1:
                                    wbf_free[slot] = (sPE, sPE.inc(i, 1))
                        last_slot[0] = slot
                    block.tensor(fchunk)
                wbf_free[last_slot[0]] = (sPE, evac_ready[LC - 1])

                def fevac(scalar, eb=eb, wname=wname, evac_ready=evac_ready):
                    base = 0 if wname == "k" else 4096
                    for l in range(LC):
                        W(scalar, sPE, evac_ready[l])
                        if wname == "k":
                            i = scalar.activation(
                                out=AB[:, base + l * 512:base + l * 512 + 512],
                                in_=bank(l), func=AF.Relu, scale=INV_SCALE)
                        else:
                            i = scalar.copy(
                                out=AB[:, base + l * 512:base + l * 512 + 512],
                                in_=bank(l))
                        bank_free[l] = (sACT, sACT.inc(i, 1))
                block.scalar(fevac)

            for st in ([] if os.environ.get("KDBG_NOSTATS") == "1" else range(4)):
                k = eb * 4 + st
                j = f"Bst{k}"
                src = d_w1s if k < 8 else d_w2s
                kk = k % 8
                wload(src[:, kk * 2048:kk * 2048 + 2048], j)

                def fstat(vector, j=j, k=k):
                    i, c = wst_ready[j]
                    W(vector, semW[i], c)
                    cc = sDVE.inc(vector.tensor_reduce(
                        out=MISC[:, THS + k:THS + k + 1], in_=wst(i),
                        axis=AX.X, op=ALU.add, apply_absolute_value=True), 1)
                    wst_free[i] = (sDVE, cc)
                block.vector(fstat)
                stats_k += 1

            def fkv(tensor, eb=eb):
                for l in range(LC):
                    W(tensor, *bank_free[l])
                last = None
                for hh in range(4):
                    for l in range(LC):
                        last = tensor.matmul(
                            bank(0)[:, hh * 128:hh * 128 + 128],
                            AB[:, l * 512 + hh * 128:l * 512 + hh * 128 + 128],
                            AB[:, 4096 + l * 512 + hh * 128:4096 + l * 512 + hh * 128 + 128],
                            start=(l == 0), stop=(l == LC - 1))
                sPE.inc(last, 1)
            block.tensor(fkv)
            kv_mm = sPE.n

            def fkve(scalar, eb=eb, kv_mm=kv_mm):
                W(scalar, sPE, kv_mm)
                kv_evac.append(sACT.inc(scalar.copy(
                    out=GA[:, eb * 512:eb * 512 + 512], in_=bank(0)), 1))
            block.scalar(fkve)
            bank_free[0] = (sACT, kv_evac[-1])

        b_stats_done = sDVE.n

        def fzrow(vector):
            sDVE.inc(vector.memset(GA[0:1, 2048:4096], 0.0), 1)
        block.vector(fzrow)
        b_zrow = sDVE.n

        def fwsum(tensor):
            W(tensor, sDVE, b_stats_done)
            W(tensor, *bank_free[2])
            sPE.inc(tensor.matmul(bank(2)[:, 0:16], onesmat,
                                  MISC[:, THS:THS + 16],
                                  start=True, stop=True), 1)
        block.tensor(fwsum)
        b_wsum_mm = sPE.n

        def fwev(scalar):
            W(scalar, sPE, b_wsum_mm)
            sACT.inc(scalar.copy(out=MISC[:, THS:THS + 16],
                                 in_=bank(2)[:, 0:16]), 1)
        block.scalar(fwev)
        b_wev = sACT.n
        bank_free[2] = (sACT, b_wev)

        def fwred(vector):
            W(vector, sACT, b_wev)
            vector.tensor_reduce(out=MISC[:, WPC:WPC + 1],
                                 in_=MISC[:, THS:THS + 8], axis=AX.X,
                                 op=ALU.add)
            sDVE.inc(vector.tensor_reduce(out=MISC[:, WPC + 1:WPC + 2],
                                          in_=MISC[:, THS + 8:THS + 16],
                                          axis=AX.X, op=ALU.add), 1)
        block.vector(fwred)
        b_wred = sDVE.n

        def fwstage(scalar):
            W(scalar, sDVE, b_wred)
            W(scalar, sDVE, b_zrow)
            sACT.inc(scalar.copy(out=GA[0:1, 2048:2050],
                                 in_=MISC[0:1, WPC:WPC + 2]), 1)
        block.scalar(fwstage)
        b_wsum = sACT.n

        def fcc_start(gpsimd):
            for c in kv_evac:
                W(gpsimd, sACT, c)
            W(gpsimd, sACT, b_wsum)
            sGD.inc(gpsimd.dma_start(out=d_ccin[0:P, :], in_=GA[:, 0:2048]), 16)
            sROW.inc(gpsimd.dma_start(out=d_ccin[P:P + 1, :],
                                      in_=GA[0:1, 2048:4096]), 16)
            W(gpsimd, sGD, sGD.n)
            W(gpsimd, sROW, sROW.n)
            i = gpsimd.collective_compute(
                "AllReduce", ALU.add, replica_groups=[list(range(NCORE))],
                ins=[t_ccin.ap().opt()], outs=[t_ccout.ap().opt()])
            sCC.inc(i, 1)
        block.gpsimd(fcc_start)

        if stop_after == "B":
            return _finish(nc, block, ALLSEMS, tap, locals())

        c_bankfree = [bank_free[0], bank_free[1], bank_free[2], bank_free[3]]
        q_done = {}
        for ec in range(H):
            j = f"Cq{ec}"
            wload(d_wq[:, ec * 2048:ec * 2048 + 2048], j)
            wcast(j)

            def fq(tensor, ec=ec, j=j):
                slot, s, c = wbf_ready[j]
                W(tensor, s, c)
                stops = []
                for lt in range(LT):
                    b = (ec * 2 + lt) % 4
                    W(tensor, *c_bankfree[b])
                    i = None
                    for dc in range(DC):
                        i = tensor.matmul(
                            bank(b), wbf(slot)[:, dc * 128:dc * 128 + 128],
                            XnT[:, dc * T + lt * 512:dc * T + lt * 512 + 512],
                            start=(dc == 0), stop=(dc == DC - 1))
                    stops.append(sPE.inc(i, 1))
                wbf_free[slot] = (sPE, stops[-1])
                q_done[("mm", ec)] = stops
            block.tensor(fq)

            def fqe(scalar, ec=ec):
                stops = q_done[("mm", ec)]
                for lt in range(LT):
                    W(scalar, sPE, stops[lt])
                    b = (ec * 2 + lt) % 4
                    c = sACT.inc(scalar.activation(
                        out=AB[:, ec * T + lt * 512:ec * T + lt * 512 + 512],
                        in_=bank(b), func=AF.Relu, scale=INV_SCALE), 1)
                    c_bankfree[b] = (sACT, c)
                q_done[ec] = c
            block.scalar(fqe)

        def fcc_fin(gpsimd):
            W(gpsimd, sCC, sCC.n)
            sGD.inc(gpsimd.dma_start(out=GA[:, 0:2048], in_=d_ccout[0:P, :]), 16)
            sTHR.inc(gpsimd.dma_start(out=MISC[0:1, WPT:WPT + 2],
                                      in_=d_ccout[P:P + 1, 0:2]), 16)
            W(gpsimd, sGD, sGD.n)
            sPOOL.inc(gpsimd.tensor_copy(out=MISCB[:, :], in_=GA[:, 0:2048]), 1)
        block.gpsimd(fcc_fin)
        b_kvg = sPOOL.n
        b_thr_ld = sTHR.n

        def fthr_rows(scalar):
            W(scalar, sTHR, b_thr_ld)
            scalar.activation(out=MISC[0:1, THS:THS + 2],
                              in_=MISC[0:1, WPT:WPT + 2], func=AF.Copy,
                              scale=THR_SCALE)
            sACT.inc(scalar.activation(out=MISC[0:1, THS + 2:THS + 4],
                                       in_=MISC[0:1, WPT:WPT + 2], func=AF.Copy,
                                       scale=-THR_SCALE), 1)
        block.scalar(fthr_rows)
        b_thr_rows = sACT.n

        def fthr_mm(tensor):
            W(tensor, sACT, b_thr_rows)
            W(tensor, *bank_free[4])
            W(tensor, *bank_free[5])
            sPE.inc(tensor.matmul(bank(4)[:, 0:4], onesrow,
                                  MISC[0:1, THS:THS + 4], start=True, stop=True), 1)
        block.tensor(fthr_mm)
        b_thr_mm = sPE.n

        def fthr_ev(scalar):
            W(scalar, sPE, b_thr_mm)
            sACT.inc(scalar.copy(out=MISC[:, THRC:THRC + 4],
                                 in_=bank(4)[:, 0:4]), 1)
        block.scalar(fthr_ev)
        b_thr = sACT.n

        if stop_after == "C":
            return _finish(nc, block, ALLSEMS, tap, locals())

        d_ps0_free = c_bankfree[0]
        d_ps0_free2 = c_bankfree[1]
        d_ps1_free = c_bankfree[2]
        d_ps1_free2 = c_bankfree[3]
        d_ps2_free = (sACT, b_thr)
        ga_gate = (sPOOL, b_kvg)
        for g in range(4):
            d1 = {}
            for hh in range(4):
                h = g * 4 + hh
                wload(d_wu[:, h * 2048:h * 2048 + 2048], f"Du{h}")
                wcast(f"Du{h}")

                def fd1(tensor, h=h, hh=hh, g=g):
                    slot, s, c = wbf_ready[f"Du{h}"]
                    W(tensor, s, c)
                    W(tensor, *d_ps0_free)
                    W(tensor, *d_ps0_free2)
                    i = None
                    for lt in range(LT):
                        for dc in range(DC):
                            i = tensor.matmul(
                                PS[0][:, lt * 512:lt * 512 + 512],
                                wbf(slot)[:, dc * 128:dc * 128 + 128],
                                XnT[:, dc * T + lt * 512:dc * T + lt * 512 + 512],
                                start=(dc == 0), stop=(dc == DC - 1))
                    d1[("u", hh)] = sPE.inc(i, 1)
                    wbf_free[slot] = (sPE, d1[("u", hh)])
                    if g == 0 and hh == 0:
                        W(tensor, sPOOL, b_kvg)
                    W(tensor, sACT, q_done[h])
                    W(tensor, *d_ps1_free)
                    W(tensor, *d_ps1_free2)
                    for lt in range(LT):
                        i = tensor.matmul(
                            PS[1][:, lt * 512:lt * 512 + 512],
                            MISCB[:, h * 128:h * 128 + 128],
                            AB[:, h * T + lt * 512:h * T + lt * 512 + 512],
                            start=True, stop=True)
                    d1[("o", hh)] = sPE.inc(i, 1)
                block.tensor(fd1)

                def fd1e(scalar, h=h, hh=hh, g=g):
                    W(scalar, sPE, d1[("u", hh)])
                    if g == 0:
                        W(scalar, *ga_gate)
                    cu = sACT.inc(scalar.activation(
                        out=GA[:, hh * 1024:hh * 1024 + 1024], in_=PS[0][:, :],
                        func=AF.Silu), 1)
                    W(scalar, sPE, d1[("o", hh)])
                    co = sACT.inc(scalar.copy(
                        out=GA[:, 4096 + hh * 1024:4096 + hh * 1024 + 1024],
                        in_=PS[1][:, :]), 1)
                    d1[("ue", hh)] = cu
                    d1[("oe", hh)] = co
                block.scalar(fd1e)
                d_ps0_free = (sACT, sACT.n - 1)
                d_ps0_free2 = (sACT, 0)
                d_ps1_free = (sACT, sACT.n)
                d_ps1_free2 = (sACT, 0)

            for hh in range(4):
                h = g * 4 + hh
                Uh = GA[:, hh * 1024:hh * 1024 + 1024]
                Oh = GA[:, 4096 + hh * 1024:4096 + hh * 1024 + 1024]
                OSQ = GA[:, 8192:9216]
                INV = GA[:, 9216:10240]

                def fsq2(vector, hh=hh, Oh=Oh, OSQ=OSQ):
                    W(vector, sACT, d1[("oe", hh)])
                    if hh == 0:
                        W(vector, sDVE, sDVE.n)
                    sDVE.inc(vector.tensor_tensor(out=OSQ, in0=Oh, in1=Oh,
                                                  op=ALU.mult), 1)
                block.vector(fsq2)
                d_osq = sDVE.n

                def fss(tensor, d_osq=d_osq, OSQ=OSQ):
                    W(tensor, sDVE, d_osq)
                    W(tensor, *d_ps2_free)
                    i = None
                    for lt in range(LT):
                        i = tensor.matmul(PS[2][:, lt * 512:lt * 512 + 512],
                                          onesmat, OSQ[:, lt * 512:lt * 512 + 512],
                                          start=True, stop=True)
                    sPE.inc(i, 1)
                block.tensor(fss)
                d_ss = sPE.n

                def finv(scalar, d_ss=d_ss, INV=INV):
                    W(scalar, sPE, d_ss)
                    sACT.inc(scalar.activation(out=INV, in_=PS[2][:, :],
                                               func=AF.Ln, bias=guard_ap), 1)
                    sACT.inc(scalar.activation(out=INV, in_=INV, func=AF.Exp,
                                               scale=-0.5, bias=expb_ap), 1)
                block.scalar(finv)
                d_ps2_free = (sACT, sACT.n - 1)
                d_inv = sACT.n

                def fgate(vector, h=h, d_inv=d_inv, Uh=Uh, Oh=Oh, OSQ=OSQ, INV=INV):
                    W(vector, sACT, d_inv)
                    c = sDVE.inc(vector.tensor_scalar_min(out=INV, in0=INV,
                                                          scalar1=INV_CLAMP), 1)
                    W(vector, sDVE, c)
                    c = sDVE.inc(vector.tensor_tensor(out=OSQ, in0=Oh, in1=INV,
                                                      op=ALU.mult), 1)
                    W(vector, sDVE, c)
                    sDVE.inc(vector.tensor_tensor(
                        out=OUTT[:, h * T:h * T + 1024], in0=OSQ, in1=Uh,
                        op=ALU.mult), 1)
                block.vector(fgate)
        d_done = sDVE.n
        d_qt_done = sPE.n

        if stop_after == "D":
            return _finish(nc, block, ALLSEMS, tap, locals())

        e_bankfree = [d_ps0_free, d_ps0_free, d_ps1_free, d_ps1_free,
                      d_ps2_free, d_ps2_free, bank_free[6], bank_free[7]]
        def ebank(l):
            return bank(l)
        e_stage_free = [(sSCD, 0), (sSCD, 0)]
        e_sdma_last = 0

        for eb in range(EB):
            evac_ready = {}
            last_slot = [0]
            for cg in range(4):
                j = f"Eo{eb}_{cg}"
                off = (eb * DC + cg * 4) * 512
                wload(d_wo[:, off:off + 2048], j)
                wcast(j)

                def fe(tensor, eb=eb, cg=cg, j=j, evac_ready=evac_ready):
                    if eb == 0 and cg == 0:
                        W(tensor, sDVE, d_done)
                    slot, s, c = wbf_ready[j]
                    W(tensor, s, c)
                    if cg == 0:
                        for l in range(8):
                            W(tensor, *e_bankfree[l])
                    for dc in range(cg * 4, cg * 4 + 4):
                        for l in range(LC):
                            i = tensor.matmul(
                                bank(l), OUTT[:, dc * T + l * 128:dc * T + l * 128 + 128],
                                wbf(slot)[:, (dc % 4) * 512:(dc % 4) * 512 + 512],
                                start=(dc == 0), stop=(dc == DC - 1))
                            if dc == DC - 1:
                                evac_ready[l] = sPE.inc(i, 1)
                            elif dc % 4 == 3 and l == LC - 1:
                                wbf_free[slot] = (sPE, sPE.inc(i, 1))
                    last_slot[0] = slot
                block.tensor(fe)
            wbf_free[last_slot[0]] = (sPE, evac_ready[LC - 1])

            def fes(scalar, eb=eb, evac_ready=evac_ready):
                W(scalar, sPE, evac_ready[LC - 1])
                W(scalar, *e_stage_free[0])
                for l in range(LC):
                    stage = GA[:, l * 512:l * 512 + 512]
                    e_bankfree[l] = (sACT, sACT.inc(
                        scalar.copy(out=stage, in_=ebank(l)), 1))
                cd = 0
                for l in range(LC):
                    stage = GA[:, l * 512:l * 512 + 512]
                    cd = semST[0].inc(scalar.dma_start(
                        out=d_s[l * 128:l * 128 + 128, eb * 512:eb * 512 + 512],
                        in_=stage), 16)
                e_stage_free[0] = (semST[0], cd)
            block.scalar(fes)
        e_sdma_last = 0
        e_st_final = (semST[0].n, semST[1].n)

        if stop_after == "E":
            return _finish(nc, block, ALLSEMS, tap, locals())

        S1 = AB[:, 8192:10240]
        S2 = AB[:, 10240:12288]
        T1B = AB[:, 12288:14336]
        thr1p = MISC[:, THRC + 0:THRC + 1]
        thr2p = MISC[:, THRC + 1:THRC + 2]
        thr1n = MISC[:, THRC + 2:THRC + 3]
        thr2n = MISC[:, THRC + 3:THRC + 4]

        e_pe_done = sPE.n
        w2q_free = [(sPE, e_pe_done)] * 8
        hT_free = [(sPE, d_qt_done)] * 8
        w2q_ready = {}
        hT_ready = {}
        w1q_ready = {}
        f_ps01_free = [(sACT, max(e_bankfree[0][1], e_bankfree[1][1])),
                       (sACT, max(e_bankfree[2][1], e_bankfree[3][1]))]
        f_ps23_free = [(sACT, max(e_bankfree[4][1], e_bankfree[5][1])),
                       (sACT, max(e_bankfree[6][1], e_bankfree[7][1]))]
        spill_dve = {}

        if os.environ.get("KDBG_NOB2") == "1":
            f_b2 = 0
        else:
            def fb2(scalar):
                W(scalar, sDVE, a_apply)
                sB2.inc(scalar.dma_start(out=ROWS[0:1, 0:2048],
                                         in_=d_b2[0:1, :]), 16)
            block.scalar(fb2)
            f_b2 = sB2.n

        first_quant = [True, True]
        f_fill_done = [0]
        _FG = FC_GROUPS
        if os.environ.get("KDBG_FGRP"):
            _FG = FC_GROUPS[:int(os.environ["KDBG_FGRP"])]
        _NODRAIN = os.environ.get("KDBG_NODRAIN") == "1"
        for gi, grp in enumerate(_FG):
            for fc in grp:
                wload(d_w2[:, fc * 2048:fc * 2048 + 2048], f"F2_{fc}")

                def fq2(gpsimd, fc=fc):
                    i, c = wst_ready[f"F2_{fc}"]
                    W(gpsimd, semW[i], c)
                    if first_quant[0]:
                        W(gpsimd, sACT, b_thr)
                        W(gpsimd, sPE, d_qt_done)
                        first_quant[0] = False
                    oslot = fc % 8
                    W(gpsimd, *w2q_free[oslot])
                    gpsimd.tensor_scalar(out=S1, in0=wst(i), scalar1=thr2n,
                                         scalar2=None, op0=ALU.is_le, op1=ALU.bypass)
                    c0 = sPOOL.inc(gpsimd.tensor_scalar(
                        out=S2, in0=wst(i), scalar1=thr2p,
                        scalar2=None, op0=ALU.is_ge, op1=ALU.bypass), 1)
                    W(gpsimd, sPOOL, c0)
                    cc = sPOOL.inc(gpsimd.tensor_tensor(
                        out=OUTT[:, oslot * 2048:oslot * 2048 + 2048],
                        in0=S2, in1=S1, op=ALU.subtract), 1)
                    wst_free[i] = (sPOOL, cc)
                    w2q_ready[fc] = (oslot, sPOOL, cc)
                if os.environ.get("KDBG_NOQ2") != "1":
                    block.gpsimd(fq2)
                else:
                    def fq2z(pool, fc=fc):
                        i, c = wst_ready[f"F2_{fc}"]
                        W(pool, semW[i], c)
                        oslot = fc % 8
                        cc = sPOOL.inc(pool.tensor_copy(
                            out=OUTT[:, oslot * 2048:oslot * 2048 + 2048],
                            in_=wst(i)), 1)
                        wst_free[i] = (sPOOL, cc)
                        w2q_ready[fc] = (oslot, sPOOL, cc)
                    block.gpsimd(fq2z)

                wload(d_w1[:, fc * 2048:fc * 2048 + 2048], f"F1_{fc}")

                def fq1(vector, fc=fc):
                    i, c = wst_ready[f"F1_{fc}"]
                    W(vector, semW[i], c)
                    if first_quant[1]:
                        W(vector, sACT, b_thr)
                        W(vector, sPE, d_qt_done)
                        first_quant[1] = False
                    j = st8["wbf"]
                    st8["wbf"] = (j + 1) % 3
                    s, n = wbf_free[j]
                    W(vector, s, n)
                    c0 = sDVE.inc(vector.tensor_scalar(
                        out=T1B, in0=wst(i), scalar1=thr1n,
                        scalar2=None, op0=ALU.is_le, op1=ALU.bypass), 1)
                    W(vector, sDVE, c0)
                    cc = sDVE.inc(vector.scalar_tensor_tensor(
                        out=wbf(j), in0=wst(i), scalar=thr1p, in1=T1B,
                        op0=ALU.is_ge, op1=ALU.subtract), 1)
                    wst_free[i] = (sDVE, cc)
                    w1q_ready[fc] = (j, sDVE, cc)
                if os.environ.get("KDBG_NOQ1") != "1":
                    block.vector(fq1)
                else:
                    def fq1z(vector, fc=fc):
                        i, c = wst_ready[f"F1_{fc}"]
                        W(vector, semW[i], c)
                        j = st8["wbf"]
                        st8["wbf"] = (j + 1) % 3
                        s, n = wbf_free[j]
                        W(vector, s, n)
                        cc = sDVE.inc(vector.tensor_copy(out=wbf(j),
                                                         in_=wst(i)), 1)
                        wst_free[i] = (sDVE, cc)
                        w1q_ready[fc] = (j, sDVE, cc)
                    block.vector(fq1z)

                def ff1(tensor, fc=fc, gi=gi):
                    j, s, c = w1q_ready[fc]
                    W(tensor, s, c)
                    pb = fc % 2
                    W(tensor, *f_ps01_free[pb])
                    i = None
                    for lt in range(LT):
                        for dc in range(DC):
                            i = tensor.matmul(
                                PS[pb][:, lt * 512:lt * 512 + 512],
                                wbf(j)[:, dc * 128:dc * 128 + 128],
                                XnT[:, dc * T + lt * 512:dc * T + lt * 512 + 512],
                                start=(dc == 0), stop=(dc == DC - 1))
                    hstop = sPE.inc(i, 1)
                    wbf_free[j] = (sPE, hstop)
                    hT_ready[fc] = ("pe", hstop)
                    f_ps01_free[pb] = (sACT, 0)
                block.tensor(ff1)

                def fge(scalar, fc=fc, gi=gi):
                    _, hstop = hT_ready[fc]
                    W(scalar, sPE, hstop)
                    hs = fc % 8
                    W(scalar, *hT_free[hs])
                    c = sACT.inc(scalar.activation(
                        out=AB[:, hs * 1024:hs * 1024 + 1024],
                        in_=PS[fc % 2][:, :], func=AF.Gelu,
                        bias=MISC[:, B1C + fc:B1C + fc + 1]), 1)
                    hT_ready[fc] = (hs, c)
                    f_ps01_free[fc % 2] = (sACT, c)
                block.scalar(fge)
            f_fill_done[0] = sPE.n

            for l in ([] if _NODRAIN else range(LC)):
                stops = {}

                def fd2(tensor, gi=gi, grp=grp, l=l, stops=stops):
                    if l == 0:
                        if gi == 0:
                            W(tensor, sB2, f_b2)
                        for fc in grp:
                            hs, c = hT_ready[fc]
                            W(tensor, sACT, c)
                            _, s2, c2 = w2q_ready[fc]
                            W(tensor, s2, c2)
                    for dh in range(2):
                        pb = 2 + dh
                        W(tensor, *f_ps23_free[dh])
                        i = None
                        if gi == 0:
                            for db in range(2):
                                tensor.matmul(
                                    PS[pb][:, db * 512:db * 512 + 512], onesrow,
                                    ROWS[0:1, dh * 1024 + db * 512:dh * 1024 + db * 512 + 512],
                                    start=True, stop=False)
                        for k, fc in enumerate(grp):
                            hs, _ = hT_ready[fc]
                            os_, _, _ = w2q_ready[fc]
                            for db in range(2):
                                i = tensor.matmul(
                                    PS[pb][:, db * 512:db * 512 + 512],
                                    AB[:, hs * 1024 + l * 128:hs * 1024 + l * 128 + 128],
                                    OUTT[:, os_ * 2048 + dh * 1024 + db * 512:
                                         os_ * 2048 + dh * 1024 + db * 512 + 512],
                                    start=(gi > 0 and k == 0),
                                    stop=(k == len(grp) - 1))
                        stops[dh] = sPE.inc(i, 1)
                        f_ps23_free[dh] = (sDVE, 0)
                block.tensor(fd2)

                def fsp(vector, gi=gi, l=l, stops=stops):
                    if gi == 0 and l == 0:
                        W(vector, semST[0], e_st_final[0])
                        W(vector, semST[1], e_st_final[1])
                    for dh in range(2):
                        W(vector, sPE, stops[dh])
                        dst = GA[:, l * 2048 + dh * 1024:l * 2048 + dh * 1024 + 1024]
                        if gi == 0:
                            i = vector.tensor_copy(out=dst, in_=PS[2 + dh][:, :])
                        else:
                            i = vector.tensor_tensor(out=dst, in0=dst,
                                                     in1=PS[2 + dh][:, :],
                                                     op=ALU.add)
                        c = sDVE.inc(i, 1)
                        f_ps23_free[dh] = (sDVE, c)
                        spill_dve[(gi, l, dh)] = c
                block.vector(fsp)

            pe_now = sPE.n
            for fc in (grp if not _NODRAIN else []):
                hs, _ = hT_ready[fc]
                os_, _, _ = w2q_ready[fc]
                hT_free[hs] = (sPE, pe_now)
                w2q_free[os_] = (sPE, pe_now)

        gi_last = len(_FG) - 1

        if stop_after == "F":
            return _finish(nc, block, ALLSEMS, tap, locals())

        XS = XnT[:, :].bitcast(F32)
        prev_add = [0]
        prev_out = [0]
        g_sil = {}
        g_ld = {}
        for l in range(LC):
            def fg1(scalar, l=l):
                if l == 0:
                    W(scalar, sPE, f_fill_done[0])
                W(scalar, sSCD, sSCD.n)
                W(scalar, sDVE, prev_add[0])
                sSCD.inc(scalar.dma_start(
                    out=XS[:, 0:2048], in_=d_xtm[:, l * 2048:l * 2048 + 2048]), 16)
                sSCD.inc(scalar.dma_start(
                    out=XS[:, 2048:4096], in_=d_s[l * 128:l * 128 + 128, :]), 16)
                W(scalar, sSCD, sSCD.n)
                W(scalar, sDVE, spill_dve[(gi_last, l, 1)])
                c3 = sACT.inc(scalar.activation(
                    out=XS[:, 4096:6144], in_=GA[:, l * 2048:l * 2048 + 2048],
                    func=AF.Silu), 1)
                g_sil[l] = c3
            block.scalar(fg1)

            def fg2(vector, l=l):
                W(vector, sACT, g_sil[l])
                c = sDVE.inc(vector.tensor_tensor(
                    out=XS[:, 6144:8192], in0=XS[:, 2048:4096],
                    in1=XS[:, 4096:6144], op=ALU.mult), 1)
                W(vector, sDVE, c)
                prev_add[0] = sDVE.inc(vector.tensor_tensor(
                    out=XS[:, 6144:8192], in0=XS[:, 6144:8192],
                    in1=XS[:, 0:2048], op=ALU.add), 1)
            block.vector(fg2)

            def fg3(scalar, l=l):
                W(scalar, sDVE, prev_add[0])
                prev_out[0] = sSCD.inc(scalar.dma_start(
                    out=d_out[:, l * 2048:l * 2048 + 2048],
                    in_=XS[:, 6144:8192]), 16)
            block.scalar(fg3)

        return _finish(nc, block, ALLSEMS, tap, locals())


def _finish(nc, block, allsems, tap, env):
    sDMA, sSCD = env["sDMA"], env["sSCD"]
    sDVE = env["sDVE"]

    if tap is not None:
        d_dbg = env["d_dbg"]
        WST = env["WST"]

        def ftap_pre(sync):
            for s in allsems:
                if s.n > 0:
                    sync.wait_ge(s.h, s.n)
        block.sync(ftap_pre)

        if tap == "s_spill":
            d_s = env["d_s"]

            def ftapd(sync):
                sDMA.inc(sync.dma_start(
                    out=d_dbg.rearrange("p (a d) -> (p a) d", a=8, d=2048),
                    in_=d_s[:, :]), 16)
            block.sync(ftapd)
            src = None
        else:
            src = env[tap]

        if src is None:
            pass
        elif src.dtype == F32:
            def ftapf(sync):
                n = src.shape[1]
                sDMA.inc(sync.dma_start(out=d_dbg[:, 0:n], in_=src[:, :]), 16)
            block.sync(ftapf)
        else:
            n = src.shape[1]
            for k in range(0, n, 2048):
                w = min(2048, n - k)

                def fcv(vector, k=k, w=w):
                    if k == 0:
                        for s in allsems:
                            if s.n > 0 and s is not sDVE:
                                vector.wait_ge(s.h, s.n)
                    else:
                        vector.wait_ge(sDMA.h, sDMA.n)
                    sDVE.inc(vector.tensor_copy(out=WST[:, 0:w],
                                                in_=src[:, k:k + w]), 1)
                block.vector(fcv)

                def fcd(sync, k=k, w=w):
                    sync.wait_ge(sDVE.h, sDVE.n)
                    sDMA.inc(sync.dma_start(out=d_dbg[:, k:k + w],
                                            in_=WST[:, 0:w]), 16)
                block.sync(fcd)

    def ffin(sync):
        for s in allsems:
            if s.n > 0:
                sync.wait_ge(s.h, s.n)
    block.sync(ffin)
    return nc


def _host_layouts(inputs):
    x = np.asarray(inputs["L0_attn_x"])
    wq = np.asarray(inputs["L0_attn_wq"])
    wk = np.asarray(inputs["L0_attn_wk"])
    wv = np.asarray(inputs["L0_attn_wv"])
    wu = np.asarray(inputs["L0_attn_wu"])
    wo = np.asarray(inputs["L0_attn_wo"])
    w1 = np.asarray(inputs["L0_ffn_w1"])
    b1 = np.asarray(inputs["L0_ffn_b1"])
    w2 = np.asarray(inputs["L0_ffn_w2"])
    b2 = np.asarray(inputs["L0_ffn_b2"])

    def rhs_layout(w):
        return np.ascontiguousarray(
            w.reshape(EB, 512, DC, P).transpose(3, 0, 2, 1).reshape(P, EB * DC * 512))

    def lhs_layout(w, nec):
        return np.ascontiguousarray(
            w.reshape(nec, 128, DC, P).transpose(3, 0, 2, 1).reshape(P, nec * DC * 128))

    def w2_layout(w):
        return np.ascontiguousarray(
            w.reshape(D, FC, P).transpose(2, 1, 0).reshape(P, FC * D))

    consts = np.zeros((P, 8), dtype=np.float32)
    consts[:, 0] = LN_EPS
    consts[:, 1] = SRMS_EXPB
    consts[:, 2] = SRMS_GUARD

    w1l = lhs_layout(w1, FC)
    w2r = w2_layout(w2)
    common = dict(
        wk_r=rhs_layout(wk), wv_r=rhs_layout(wv), wo_r=rhs_layout(wo),
        wq_l=lhs_layout(wq, H), wu_l=lhs_layout(wu, H),
        w1_l=w1l, w2_r=w2r,
        b1r=np.ascontiguousarray(b1.reshape(FC, P).T),
        b2row=np.ascontiguousarray(b2.reshape(1, D)),
        consts=consts,
    )

    in_maps = []
    for c in range(NCORE):
        xc = x[0, c * T:(c + 1) * T, :]
        xT = np.ascontiguousarray(
            xc.reshape(T, DC, P).transpose(2, 1, 0).reshape(P, DC * T))
        xtm = np.ascontiguousarray(
            xc.reshape(LC, P, D).transpose(1, 0, 2).reshape(P, LC * D))
        m = dict(common)
        m["xT"] = xT
        m["xtm"] = xtm
        m["w1slice"] = np.ascontiguousarray(w1l[:, c * 16384:(c + 1) * 16384])
        m["w2slice"] = np.ascontiguousarray(w2r[:, c * 16384:(c + 1) * 16384])
        in_maps.append(m)
    return in_maps


def kernel(**inputs):
    in_maps = _host_layouts(inputs)
    nc = build_full()
    res = run_bass_kernel_spmd(nc, in_maps, core_ids=list(range(NCORE)))
    outs = []
    for c in range(NCORE):
        o = np.asarray(res.results[c]["out"])
        outs.append(o.reshape(P, LC, D).transpose(1, 0, 2).reshape(T, D))
    return np.concatenate(outs, axis=0).reshape(1, 8192, D)


# version 03
# speedup vs baseline: 2.0045x; 2.0045x over previous
import math
import os
import numpy as np
from contextlib import ExitStack

import concourse.bass as bass
import concourse.mybir as mybir
from concourse.bass_utils import run_bass_kernel_spmd

F32 = mybir.dt.float32
BF16 = mybir.dt.bfloat16
AF = mybir.ActivationFunctionType
ALU = mybir.AluOpType
AX = mybir.AxisListType

P = 128
T = 1024
LC = 8
LT = 2
D = 2048
DC = 16
FF = 8192
FC = 64
H = 16
EB = 4
NCORE = 8

INV_SCALE = 1.0 / math.sqrt(128.0)
SRMS_EXPB = 3.5 * math.log(2.0)
LN_EPS = 1e-5
SRMS_GUARD = 1e-30
INV_CLAMP = 1e12
THR_SCALE = 0.7 / float(8192 * 2048)

GSIZE = 6
FC_GROUPS = [list(range(g, min(g + GSIZE, FC))) for g in range(0, FC, GSIZE)]

B1C = 0
THRC = 64
WPC = 68
WPT = 70
OCC = 72
CSTC = 72
ONES = 80
THS = 208


class Sem:
    __slots__ = ("h", "n")

    def __init__(self, handle):
        self.h = handle
        self.n = 0

    def inc(self, instr, k=1):
        instr.then_inc(self.h, k)
        self.n += k
        return self.n


def build_full(stop_after=None, tap=None):
    nc = bass.Bass("TRN2", target_bir_lowering=False)

    d_xT = nc.dram_tensor("xT", [P, DC * T], F32, kind="ExternalInput").ap()
    d_xtm = nc.dram_tensor("xtm", [P, LC * D], F32, kind="ExternalInput").ap()
    d_wk = nc.dram_tensor("wk_r", [P, EB * DC * 512], F32, kind="ExternalInput").ap()
    d_wv = nc.dram_tensor("wv_r", [P, EB * DC * 512], F32, kind="ExternalInput").ap()
    d_wo = nc.dram_tensor("wo_r", [P, EB * DC * 512], F32, kind="ExternalInput").ap()
    d_wq = nc.dram_tensor("wq_l", [P, H * DC * 128], F32, kind="ExternalInput").ap()
    d_wu = nc.dram_tensor("wu_l", [P, H * DC * 128], F32, kind="ExternalInput").ap()
    d_w1 = nc.dram_tensor("w1_l", [P, FC * DC * 128], F32, kind="ExternalInput").ap()
    d_w2 = nc.dram_tensor("w2_r", [P, FC * D], F32, kind="ExternalInput").ap()
    d_w1s = nc.dram_tensor("w1slice", [P, 8 * 2048], F32, kind="ExternalInput").ap()
    d_w2s = nc.dram_tensor("w2slice", [P, 8 * 2048], F32, kind="ExternalInput").ap()
    d_b1 = nc.dram_tensor("b1r", [P, FC], F32, kind="ExternalInput").ap()
    d_b2 = nc.dram_tensor("b2row", [1, D], F32, kind="ExternalInput").ap()
    d_cst = nc.dram_tensor("consts", [P, 8], F32, kind="ExternalInput").ap()

    d_out = nc.dram_tensor("out", [P, LC * D], F32, kind="ExternalOutput").ap()
    d_dbg = None
    if tap is not None:
        d_dbg = nc.dram_tensor("dbg", [P, DC * T], F32, kind="ExternalOutput").ap()

    d_s = nc.dram_tensor("s_spill", [T, D], F32).ap()
    t_ccin = nc.dram_tensor("cc_in", [P + 1, D], F32)
    t_ccout = nc.dram_tensor("cc_out", [P + 1, D], F32)
    d_ccin = t_ccin.ap()
    d_ccout = t_ccout.ap()

    with ExitStack() as ctx:
        En = ctx.enter_context
        XnT = En(nc.sbuf_tensor("XnT", [P, DC * T], BF16))
        AB = En(nc.sbuf_tensor("AB", [P, DC * T], BF16))
        OUTT = En(nc.sbuf_tensor("OUTT", [P, DC * T], BF16))
        GA = En(nc.sbuf_tensor("GA", [P, DC * T], F32))
        WST = En(nc.sbuf_tensor("WST", [P, 2 * 2048], F32))
        WBF = En(nc.sbuf_tensor("WBF", [P, 3 * 2048], BF16))
        MISC = En(nc.sbuf_tensor("MISC", [P, 224], F32))
        MISCB = En(nc.sbuf_tensor("MISCB", [P, D], BF16))
        ROWS = En(nc.sbuf_tensor("ROWS", [P, 3072], F32))
        PS = [En(nc.psum_tensor(f"PS{i}", [P, 1024], F32)) for i in range(4)]

        sDMA = Sem(En(nc.semaphore("sDMA")))
        sSCD = Sem(En(nc.semaphore("sSCD")))
        sGD = Sem(En(nc.semaphore("sGD")))
        sPE = Sem(En(nc.semaphore("sPE")))
        sDVE = Sem(En(nc.semaphore("sDVE")))
        sACT = Sem(En(nc.semaphore("sACT")))
        sPOOL = Sem(En(nc.semaphore("sPOOL")))
        sCC = Sem(En(nc.semaphore("sCC")))
        sB2 = Sem(En(nc.semaphore("sB2")))
        sROW = Sem(En(nc.semaphore("sROW")))
        sTHR = Sem(En(nc.semaphore("sTHR")))
        sW0 = Sem(En(nc.semaphore("sW0")))
        sW1 = Sem(En(nc.semaphore("sW1")))
        semW = [sW0, sW1]
        sST0 = Sem(En(nc.semaphore("sST0")))
        sST1 = Sem(En(nc.semaphore("sST1")))
        semST = [sST0, sST1]
        ALLSEMS = [sDMA, sSCD, sGD, sPE, sDVE, sACT, sPOOL, sCC, sB2, sROW,
                   sTHR, sW0, sW1, sST0, sST1]

        block = En(nc.Block())

        def W(eng, sem, n):
            if n > 0:
                eng.wait_ge(sem.h, n)

        def wst(i):
            return WST[:, i * 2048:(i + 1) * 2048]

        def wbf(i):
            return WBF[:, i * 2048:(i + 1) * 2048]

        def bank(i):
            return PS[i // 2][:, (i % 2) * 512:(i % 2) * 512 + 512]

        onesrow = MISC[0:1, ONES:ONES + 128]
        onesmat = MISC[:, ONES:ONES + 128]
        eps_ap = MISC[:, CSTC + 0:CSTC + 1]
        expb_ap = MISC[:, CSTC + 1:CSTC + 2]
        guard_ap = MISC[:, CSTC + 2:CSTC + 3]

        wst_free = [(sDVE, 0), (sDVE, 0)]
        wbf_free = [(sPE, 0), (sPE, 0), (sPE, 0)]
        st8 = {"wst": 1, "wbf": 0}
        wst_ready = {}
        wbf_ready = {}

        def wload(src_ap, jid):
            i = st8["wst"]
            st8["wst"] = 1 - i

            def f(sync, i=i, src_ap=src_ap, jid=jid):
                s, n = wst_free[i]
                W(sync, s, n)
                c = semW[i].inc(sync.dma_start(out=wst(i), in_=src_ap), 16)
                wst_ready[jid] = (i, c)
            block.sync(f)

        def wcast(jid):
            j = st8["wbf"]
            st8["wbf"] = (j + 1) % 3

            def f(scalar, j=j, jid=jid):
                i, c = wst_ready[jid]
                s, n = wbf_free[j]
                W(scalar, s, n)
                W(scalar, semW[i], c)
                cc = sACT.inc(scalar.copy(out=wbf(j), in_=wst(i)), 1)
                wst_free[i] = (sACT, cc)
                wbf_ready[jid] = (j, sACT, cc)
            block.scalar(f)

        def fA_loads(sync):
            sDMA.inc(sync.dma_start(out=GA[:, 0:8192], in_=d_xT[:, 0:8192]), 16)
            sDMA.inc(sync.dma_start(out=GA[:, 8192:16384], in_=d_xT[:, 8192:16384]), 16)
            sDMA.inc(sync.dma_start(out=MISC[:, CSTC:CSTC + 8], in_=d_cst[:, :]), 16)
            sDMA.inc(sync.dma_start(out=MISC[:, B1C:B1C + FC], in_=d_b1[:, :]), 16)
        block.sync(fA_loads)
        aload = sDMA.n

        def fA_memset(vector):
            sDVE.inc(vector.memset(MISC[:, ONES:ONES + 128], 1.0), 1)
        block.vector(fA_memset)
        v_ones = sDVE.n

        sumsq_mm = []
        sq_cnt = []
        for dc in range(DC):
            def fsq(scalar, dc=dc):
                if dc == 0:
                    W(scalar, sDMA, aload)
                if dc >= 2:
                    W(scalar, sPE, sumsq_mm[dc - 2])
                c = sACT.inc(scalar.activation(
                    out=ROWS[:, (dc % 2) * 1024:(dc % 2) * 1024 + 1024],
                    in_=GA[:, dc * T:dc * T + 1024], func=AF.Square), 1)
                sq_cnt.append(c)
            block.scalar(fsq)

            def fmm(tensor, dc=dc):
                if dc == 0:
                    W(tensor, sDMA, aload)
                    W(tensor, sDVE, v_ones)
                for lt in range(LT):
                    tensor.matmul(bank(lt), onesmat,
                                  GA[:, dc * T + lt * 512: dc * T + lt * 512 + 512],
                                  start=(dc == 0), stop=(dc == DC - 1))
                W(tensor, sACT, sq_cnt[dc])
                i = None
                for lt in range(LT):
                    i = tensor.matmul(
                        bank(2 + lt), onesmat,
                        ROWS[:, (dc % 2) * 1024 + lt * 512:(dc % 2) * 1024 + lt * 512 + 512],
                        start=(dc == 0), stop=(dc == DC - 1))
                sumsq_mm.append(sPE.inc(i, 1))
            block.tensor(fmm)
        a_stats_mm = sPE.n

        MB = ROWS[:, 0:1024]
        EXB = ROWS[:, 1024:2048]
        SCB = ROWS[:, 2048:3072]

        def fA_evac(scalar):
            W(scalar, sPE, a_stats_mm)
            for lt in range(LT):
                scalar.activation(out=MB[:, lt * 512:lt * 512 + 512],
                                  in_=bank(lt), func=AF.Copy, scale=1.0 / D)
            for lt in range(LT):
                sACT.inc(scalar.activation(out=EXB[:, lt * 512:lt * 512 + 512],
                                           in_=bank(2 + lt), func=AF.Copy,
                                           scale=1.0 / D), 1)
        block.scalar(fA_evac)
        a_mrow = sACT.n

        def fA_var(vector):
            W(vector, sACT, a_mrow)
            c = sDVE.inc(vector.tensor_tensor(out=SCB, in0=MB, in1=MB,
                                              op=ALU.mult), 1)
            W(vector, sDVE, c)
            sDVE.inc(vector.tensor_tensor(out=SCB, in0=EXB, in1=SCB,
                                          op=ALU.subtract), 1)
        block.vector(fA_var)
        a_var = sDVE.n

        def fA_rstd(scalar):
            W(scalar, sDVE, a_var)
            scalar.activation(out=EXB, in_=SCB, func=AF.Ln, bias=eps_ap)
            sACT.inc(scalar.activation(out=SCB, in_=EXB, func=AF.Exp,
                                       scale=-0.5), 1)
        block.scalar(fA_rstd)
        a_rstd = sACT.n

        apply_cnt = []
        def fA_apply(vector):
            W(vector, sACT, a_rstd)
            for dc in range(DC):
                tmp = WST[:, (dc % 2) * 1024:(dc % 2) * 1024 + 1024]
                c = sDVE.inc(vector.tensor_tensor(
                    out=tmp, in0=GA[:, dc * T:dc * T + 1024],
                    in1=MB, op=ALU.subtract), 1)
                W(vector, sDVE, c)
                apply_cnt.append(sDVE.inc(vector.tensor_tensor(
                    out=XnT[:, dc * T:dc * T + 1024], in0=tmp, in1=SCB,
                    op=ALU.mult), 1))
        block.vector(fA_apply)
        a_apply = sDVE.n
        wst_free[0] = (sDVE, a_apply)
        wst_free[1] = (sDVE, 0)

        if stop_after == "A":
            return _finish(nc, block, ALLSEMS, tap, locals())

        bank_free = [(sACT, a_mrow)] * 4 + [(sDVE, 0)] * 4
        kv_evac = []
        stats_k = 0

        for eb in range(EB):
            for wname, dap in (("k", d_wk), ("v", d_wv)):
                evac_ready = {}
                last_slot = [0]
                for cg in range(4):
                    j = f"B{wname}{eb}_{cg}"
                    off = (eb * DC + cg * 4) * 512
                    wload(dap[:, off:off + 2048], j)
                    wcast(j)

                    def fchunk(tensor, eb=eb, wname=wname, cg=cg, j=j,
                               evac_ready=evac_ready):
                        if eb == 0 and wname == "k" and cg == 0:
                            W(tensor, sDVE, a_apply)
                        slot, s, c = wbf_ready[j]
                        W(tensor, s, c)
                        if cg == 0:
                            for l in range(8):
                                W(tensor, *bank_free[l])
                        for dc in range(cg * 4, cg * 4 + 4):
                            for l in range(LC):
                                i = tensor.matmul(
                                    bank(l),
                                    XnT[:, dc * T + l * 128: dc * T + l * 128 + 128],
                                    wbf(slot)[:, (dc % 4) * 512:(dc % 4) * 512 + 512],
                                    start=(dc == 0), stop=(dc == DC - 1))
                                if dc == DC - 1:
                                    evac_ready[l] = sPE.inc(i, 1)
                                elif dc % 4 == 3 and l == LC - 1:
                                    wbf_free[slot] = (sPE, sPE.inc(i, 1))
                        last_slot[0] = slot
                    block.tensor(fchunk)
                wbf_free[last_slot[0]] = (sPE, evac_ready[LC - 1])

                def fevac(scalar, eb=eb, wname=wname, evac_ready=evac_ready):
                    base = 0 if wname == "k" else 4096
                    for l in range(LC):
                        W(scalar, sPE, evac_ready[l])
                        if wname == "k":
                            i = scalar.activation(
                                out=AB[:, base + l * 512:base + l * 512 + 512],
                                in_=bank(l), func=AF.Relu, scale=INV_SCALE)
                        else:
                            i = scalar.copy(
                                out=AB[:, base + l * 512:base + l * 512 + 512],
                                in_=bank(l))
                        bank_free[l] = (sACT, sACT.inc(i, 1))
                block.scalar(fevac)

            for st in ([] if os.environ.get("KDBG_NOSTATS") == "1" else range(4)):
                k = eb * 4 + st
                j = f"Bst{k}"
                src = d_w1s if k < 8 else d_w2s
                kk = k % 8
                wload(src[:, kk * 2048:kk * 2048 + 2048], j)

                def fstat(vector, j=j, k=k):
                    i, c = wst_ready[j]
                    W(vector, semW[i], c)
                    cc = sDVE.inc(vector.tensor_reduce(
                        out=MISC[:, THS + k:THS + k + 1], in_=wst(i),
                        axis=AX.X, op=ALU.add, apply_absolute_value=True), 1)
                    wst_free[i] = (sDVE, cc)
                block.vector(fstat)
                stats_k += 1

            def fkv(tensor, eb=eb):
                for l in range(LC):
                    W(tensor, *bank_free[l])
                last = None
                for hh in range(4):
                    for l in range(LC):
                        last = tensor.matmul(
                            bank(0)[:, hh * 128:hh * 128 + 128],
                            AB[:, l * 512 + hh * 128:l * 512 + hh * 128 + 128],
                            AB[:, 4096 + l * 512 + hh * 128:4096 + l * 512 + hh * 128 + 128],
                            start=(l == 0), stop=(l == LC - 1))
                sPE.inc(last, 1)
            block.tensor(fkv)
            kv_mm = sPE.n

            def fkve(scalar, eb=eb, kv_mm=kv_mm):
                W(scalar, sPE, kv_mm)
                kv_evac.append(sACT.inc(scalar.copy(
                    out=GA[:, eb * 512:eb * 512 + 512], in_=bank(0)), 1))
            block.scalar(fkve)
            bank_free[0] = (sACT, kv_evac[-1])

        b_stats_done = sDVE.n

        def fzrow(vector):
            sDVE.inc(vector.memset(GA[0:1, 2048:4096], 0.0), 1)
        block.vector(fzrow)
        b_zrow = sDVE.n

        def fwsum(tensor):
            W(tensor, sDVE, b_stats_done)
            W(tensor, *bank_free[2])
            sPE.inc(tensor.matmul(bank(2)[:, 0:16], onesmat,
                                  MISC[:, THS:THS + 16],
                                  start=True, stop=True), 1)
        block.tensor(fwsum)
        b_wsum_mm = sPE.n

        def fwev(scalar):
            W(scalar, sPE, b_wsum_mm)
            sACT.inc(scalar.copy(out=MISC[:, THS:THS + 16],
                                 in_=bank(2)[:, 0:16]), 1)
        block.scalar(fwev)
        b_wev = sACT.n
        bank_free[2] = (sACT, b_wev)

        def fwred(vector):
            W(vector, sACT, b_wev)
            vector.tensor_reduce(out=MISC[:, WPC:WPC + 1],
                                 in_=MISC[:, THS:THS + 8], axis=AX.X,
                                 op=ALU.add)
            sDVE.inc(vector.tensor_reduce(out=MISC[:, WPC + 1:WPC + 2],
                                          in_=MISC[:, THS + 8:THS + 16],
                                          axis=AX.X, op=ALU.add), 1)
        block.vector(fwred)
        b_wred = sDVE.n

        def fwstage(scalar):
            W(scalar, sDVE, b_wred)
            W(scalar, sDVE, b_zrow)
            sACT.inc(scalar.copy(out=GA[0:1, 2048:2050],
                                 in_=MISC[0:1, WPC:WPC + 2]), 1)
        block.scalar(fwstage)
        b_wsum = sACT.n

        def fcc_start(gpsimd):
            for c in kv_evac:
                W(gpsimd, sACT, c)
            W(gpsimd, sACT, b_wsum)
            sGD.inc(gpsimd.dma_start(out=d_ccin[0:P, :], in_=GA[:, 0:2048]), 16)
            sROW.inc(gpsimd.dma_start(out=d_ccin[P:P + 1, :],
                                      in_=GA[0:1, 2048:4096]), 16)
            W(gpsimd, sGD, sGD.n)
            W(gpsimd, sROW, sROW.n)
            i = gpsimd.collective_compute(
                "AllReduce", ALU.add, replica_groups=[list(range(NCORE))],
                ins=[t_ccin.ap().opt()], outs=[t_ccout.ap().opt()])
            sCC.inc(i, 1)
        block.gpsimd(fcc_start)

        if stop_after == "B":
            return _finish(nc, block, ALLSEMS, tap, locals())

        c_bankfree = [bank_free[0], bank_free[1], bank_free[2], bank_free[3]]
        q_done = {}
        for ec in range(H):
            j = f"Cq{ec}"
            wload(d_wq[:, ec * 2048:ec * 2048 + 2048], j)
            wcast(j)

            def fq(tensor, ec=ec, j=j):
                slot, s, c = wbf_ready[j]
                W(tensor, s, c)
                stops = []
                for lt in range(LT):
                    b = (ec * 2 + lt) % 4
                    W(tensor, *c_bankfree[b])
                    i = None
                    for dc in range(DC):
                        i = tensor.matmul(
                            bank(b), wbf(slot)[:, dc * 128:dc * 128 + 128],
                            XnT[:, dc * T + lt * 512:dc * T + lt * 512 + 512],
                            start=(dc == 0), stop=(dc == DC - 1))
                    stops.append(sPE.inc(i, 1))
                wbf_free[slot] = (sPE, stops[-1])
                q_done[("mm", ec)] = stops
            block.tensor(fq)

            def fqe(scalar, ec=ec):
                stops = q_done[("mm", ec)]
                for lt in range(LT):
                    W(scalar, sPE, stops[lt])
                    b = (ec * 2 + lt) % 4
                    c = sACT.inc(scalar.activation(
                        out=AB[:, ec * T + lt * 512:ec * T + lt * 512 + 512],
                        in_=bank(b), func=AF.Relu, scale=INV_SCALE), 1)
                    c_bankfree[b] = (sACT, c)
                q_done[ec] = c
            block.scalar(fqe)

        def fcc_fin(gpsimd):
            W(gpsimd, sCC, sCC.n)
            sGD.inc(gpsimd.dma_start(out=GA[:, 0:2048], in_=d_ccout[0:P, :]), 16)
            sTHR.inc(gpsimd.dma_start(out=MISC[0:1, WPT:WPT + 2],
                                      in_=d_ccout[P:P + 1, 0:2]), 16)
            W(gpsimd, sGD, sGD.n)
            sPOOL.inc(gpsimd.tensor_copy(out=MISCB[:, :], in_=GA[:, 0:2048]), 1)
        block.gpsimd(fcc_fin)
        b_kvg = sPOOL.n
        b_thr_ld = sTHR.n

        def fthr_rows(scalar):
            W(scalar, sTHR, b_thr_ld)
            scalar.activation(out=MISC[0:1, THS:THS + 2],
                              in_=MISC[0:1, WPT:WPT + 2], func=AF.Copy,
                              scale=THR_SCALE)
            sACT.inc(scalar.activation(out=MISC[0:1, THS + 2:THS + 4],
                                       in_=MISC[0:1, WPT:WPT + 2], func=AF.Copy,
                                       scale=-THR_SCALE), 1)
        block.scalar(fthr_rows)
        b_thr_rows = sACT.n

        def fthr_mm(tensor):
            W(tensor, sACT, b_thr_rows)
            W(tensor, *bank_free[4])
            W(tensor, *bank_free[5])
            sPE.inc(tensor.matmul(bank(4)[:, 0:4], onesrow,
                                  MISC[0:1, THS:THS + 4], start=True, stop=True), 1)
        block.tensor(fthr_mm)
        b_thr_mm = sPE.n

        def fthr_ev(scalar):
            W(scalar, sPE, b_thr_mm)
            sACT.inc(scalar.copy(out=MISC[:, THRC:THRC + 4],
                                 in_=bank(4)[:, 0:4]), 1)
        block.scalar(fthr_ev)
        b_thr = sACT.n

        if stop_after == "C":
            return _finish(nc, block, ALLSEMS, tap, locals())

        d_ps0_free = c_bankfree[0]
        d_ps0_free2 = c_bankfree[1]
        d_ps1_free = c_bankfree[2]
        d_ps1_free2 = c_bankfree[3]
        d_ps2_free = (sACT, b_thr)
        ga_gate = (sPOOL, b_kvg)
        for g in range(4):
            d1 = {}
            for hh in range(4):
                h = g * 4 + hh
                wload(d_wu[:, h * 2048:h * 2048 + 2048], f"Du{h}")
                wcast(f"Du{h}")

                def fd1(tensor, h=h, hh=hh, g=g):
                    slot, s, c = wbf_ready[f"Du{h}"]
                    W(tensor, s, c)
                    W(tensor, *d_ps0_free)
                    W(tensor, *d_ps0_free2)
                    i = None
                    for lt in range(LT):
                        for dc in range(DC):
                            i = tensor.matmul(
                                PS[0][:, lt * 512:lt * 512 + 512],
                                wbf(slot)[:, dc * 128:dc * 128 + 128],
                                XnT[:, dc * T + lt * 512:dc * T + lt * 512 + 512],
                                start=(dc == 0), stop=(dc == DC - 1))
                    d1[("u", hh)] = sPE.inc(i, 1)
                    wbf_free[slot] = (sPE, d1[("u", hh)])
                    if g == 0 and hh == 0:
                        W(tensor, sPOOL, b_kvg)
                    W(tensor, sACT, q_done[h])
                    W(tensor, *d_ps1_free)
                    W(tensor, *d_ps1_free2)
                    for lt in range(LT):
                        i = tensor.matmul(
                            PS[1][:, lt * 512:lt * 512 + 512],
                            MISCB[:, h * 128:h * 128 + 128],
                            AB[:, h * T + lt * 512:h * T + lt * 512 + 512],
                            start=True, stop=True)
                    d1[("o", hh)] = sPE.inc(i, 1)
                block.tensor(fd1)

                def fd1e(scalar, h=h, hh=hh, g=g):
                    W(scalar, sPE, d1[("u", hh)])
                    if g == 0:
                        W(scalar, *ga_gate)
                    cu = sACT.inc(scalar.activation(
                        out=GA[:, hh * 1024:hh * 1024 + 1024], in_=PS[0][:, :],
                        func=AF.Silu), 1)
                    W(scalar, sPE, d1[("o", hh)])
                    co = sACT.inc(scalar.copy(
                        out=GA[:, 4096 + hh * 1024:4096 + hh * 1024 + 1024],
                        in_=PS[1][:, :]), 1)
                    d1[("ue", hh)] = cu
                    d1[("oe", hh)] = co
                block.scalar(fd1e)
                d_ps0_free = (sACT, sACT.n - 1)
                d_ps0_free2 = (sACT, 0)
                d_ps1_free = (sACT, sACT.n)
                d_ps1_free2 = (sACT, 0)

            for hh in range(4):
                h = g * 4 + hh
                Uh = GA[:, hh * 1024:hh * 1024 + 1024]
                Oh = GA[:, 4096 + hh * 1024:4096 + hh * 1024 + 1024]
                OSQ = GA[:, 8192:9216]
                INV = GA[:, 9216:10240]

                def fsq2(vector, hh=hh, Oh=Oh, OSQ=OSQ):
                    W(vector, sACT, d1[("oe", hh)])
                    if hh == 0:
                        W(vector, sDVE, sDVE.n)
                    sDVE.inc(vector.tensor_tensor(out=OSQ, in0=Oh, in1=Oh,
                                                  op=ALU.mult), 1)
                block.vector(fsq2)
                d_osq = sDVE.n

                def fss(tensor, d_osq=d_osq, OSQ=OSQ):
                    W(tensor, sDVE, d_osq)
                    W(tensor, *d_ps2_free)
                    i = None
                    for lt in range(LT):
                        i = tensor.matmul(PS[2][:, lt * 512:lt * 512 + 512],
                                          onesmat, OSQ[:, lt * 512:lt * 512 + 512],
                                          start=True, stop=True)
                    sPE.inc(i, 1)
                block.tensor(fss)
                d_ss = sPE.n

                def finv(scalar, d_ss=d_ss, INV=INV):
                    W(scalar, sPE, d_ss)
                    sACT.inc(scalar.activation(out=INV, in_=PS[2][:, :],
                                               func=AF.Ln, bias=guard_ap), 1)
                    sACT.inc(scalar.activation(out=INV, in_=INV, func=AF.Exp,
                                               scale=-0.5, bias=expb_ap), 1)
                block.scalar(finv)
                d_ps2_free = (sACT, sACT.n - 1)
                d_inv = sACT.n

                def fgate(vector, h=h, d_inv=d_inv, Uh=Uh, Oh=Oh, OSQ=OSQ, INV=INV):
                    W(vector, sACT, d_inv)
                    c = sDVE.inc(vector.tensor_scalar_min(out=INV, in0=INV,
                                                          scalar1=INV_CLAMP), 1)
                    W(vector, sDVE, c)
                    c = sDVE.inc(vector.tensor_tensor(out=OSQ, in0=Oh, in1=INV,
                                                      op=ALU.mult), 1)
                    W(vector, sDVE, c)
                    sDVE.inc(vector.tensor_tensor(
                        out=OUTT[:, h * T:h * T + 1024], in0=OSQ, in1=Uh,
                        op=ALU.mult), 1)
                block.vector(fgate)
        d_done = sDVE.n
        d_qt_done = sPE.n

        if stop_after == "D":
            return _finish(nc, block, ALLSEMS, tap, locals())

        e_bankfree = [d_ps0_free, d_ps0_free, d_ps1_free, d_ps1_free,
                      d_ps2_free, d_ps2_free, bank_free[6], bank_free[7]]
        def ebank(l):
            return bank(l)
        e_stage_free = [(sSCD, 0), (sSCD, 0)]
        e_sdma_last = 0

        for eb in range(EB):
            evac_ready = {}
            last_slot = [0]
            for cg in range(4):
                j = f"Eo{eb}_{cg}"
                off = (eb * DC + cg * 4) * 512
                wload(d_wo[:, off:off + 2048], j)
                wcast(j)

                def fe(tensor, eb=eb, cg=cg, j=j, evac_ready=evac_ready):
                    if eb == 0 and cg == 0:
                        W(tensor, sDVE, d_done)
                    slot, s, c = wbf_ready[j]
                    W(tensor, s, c)
                    if cg == 0:
                        for l in range(8):
                            W(tensor, *e_bankfree[l])
                    for dc in range(cg * 4, cg * 4 + 4):
                        for l in range(LC):
                            i = tensor.matmul(
                                bank(l), OUTT[:, dc * T + l * 128:dc * T + l * 128 + 128],
                                wbf(slot)[:, (dc % 4) * 512:(dc % 4) * 512 + 512],
                                start=(dc == 0), stop=(dc == DC - 1))
                            if dc == DC - 1:
                                evac_ready[l] = sPE.inc(i, 1)
                            elif dc % 4 == 3 and l == LC - 1:
                                wbf_free[slot] = (sPE, sPE.inc(i, 1))
                    last_slot[0] = slot
                block.tensor(fe)
            wbf_free[last_slot[0]] = (sPE, evac_ready[LC - 1])

            def fes(scalar, eb=eb, evac_ready=evac_ready):
                W(scalar, sPE, evac_ready[LC - 1])
                W(scalar, *e_stage_free[0])
                for l in range(LC):
                    stage = GA[:, l * 512:l * 512 + 512]
                    e_bankfree[l] = (sACT, sACT.inc(
                        scalar.copy(out=stage, in_=ebank(l)), 1))
                cd = 0
                for l in range(LC):
                    stage = GA[:, l * 512:l * 512 + 512]
                    cd = semST[0].inc(scalar.dma_start(
                        out=d_s[l * 128:l * 128 + 128, eb * 512:eb * 512 + 512],
                        in_=stage), 16)
                e_stage_free[0] = (semST[0], cd)
            block.scalar(fes)
        e_sdma_last = 0
        e_st_final = (semST[0].n, semST[1].n)

        if stop_after == "E":
            return _finish(nc, block, ALLSEMS, tap, locals())

        QSA = AB[:, 8192:12288].bitcast(F32)
        QSB = AB[:, 12288:16384].bitcast(F32)
        thr1p = MISC[:, THRC + 0:THRC + 1]
        thr2p = MISC[:, THRC + 1:THRC + 2]
        thr1n = MISC[:, THRC + 2:THRC + 3]
        thr2n = MISC[:, THRC + 3:THRC + 4]

        e_pe_done = sPE.n
        w2q_free = [(sPE, e_pe_done)] * 8
        hT_free = [(sPE, d_qt_done)] * 8
        w2q_ready = {}
        hT_ready = {}
        w1q_ready = {}
        f_ps01_free = [(sACT, max(e_bankfree[0][1], e_bankfree[1][1])),
                       (sACT, max(e_bankfree[2][1], e_bankfree[3][1]))]
        f_ps23_free = [(sACT, max(e_bankfree[4][1], e_bankfree[5][1])),
                       (sACT, max(e_bankfree[6][1], e_bankfree[7][1]))]
        spill_dve = {}

        if os.environ.get("KDBG_NOB2") == "1":
            f_b2 = 0
        else:
            def fb2(scalar):
                W(scalar, sDVE, a_apply)
                sB2.inc(scalar.dma_start(out=ROWS[0:1, 0:2048],
                                         in_=d_b2[0:1, :]), 16)
            block.scalar(fb2)
            f_b2 = sB2.n

        first_quant = [True, True]
        q1_pre = {}
        q2_pre = {}
        q1_scr_free = [(sACT, 0)]
        q2_scr_free = [(sACT, 0)]
        f_fill_done = [0]
        _FG = FC_GROUPS
        if os.environ.get("KDBG_FGRP"):
            _FG = FC_GROUPS[:int(os.environ["KDBG_FGRP"])]
        _NODRAIN = os.environ.get("KDBG_NODRAIN") == "1"
        for gi, grp in enumerate(_FG):
            for fc in grp:
                wload(d_w2[:, fc * 2048:fc * 2048 + 2048], f"F2_{fc}")

                def fq2(vector, fc=fc):
                    i, c = wst_ready[f"F2_{fc}"]
                    W(vector, semW[i], c)
                    if first_quant[0]:
                        W(vector, sACT, b_thr)
                        W(vector, sPE, d_qt_done)
                        first_quant[0] = False
                    W(vector, *q2_scr_free[0])
                    c0 = sDVE.inc(vector.tensor_scalar(
                        out=QSA, in0=wst(i), scalar1=thr2n, scalar2=thr2p,
                        op0=ALU.max, op1=ALU.min), 1)
                    W(vector, sDVE, c0)
                    cc = sDVE.inc(vector.tensor_tensor(
                        out=QSA, in0=wst(i), in1=QSA, op=ALU.subtract), 1)
                    wst_free[i] = (sDVE, cc)
                    q2_pre[fc] = cc
                block.vector(fq2)

                def fq2s(scalar, fc=fc):
                    oslot = fc % 8
                    W(scalar, sDVE, q2_pre[fc])
                    W(scalar, *w2q_free[oslot])
                    cc = sACT.inc(scalar.activation(
                        out=OUTT[:, oslot * 2048:oslot * 2048 + 2048],
                        in_=QSA, func=AF.Sign), 1)
                    q2_scr_free[0] = (sACT, cc)
                    w2q_ready[fc] = (oslot, sACT, cc)
                block.scalar(fq2s)

                wload(d_w1[:, fc * 2048:fc * 2048 + 2048], f"F1_{fc}")

                def fq1(vector, fc=fc):
                    i, c = wst_ready[f"F1_{fc}"]
                    W(vector, semW[i], c)
                    if first_quant[1]:
                        W(vector, sACT, b_thr)
                        W(vector, sPE, d_qt_done)
                        first_quant[1] = False
                    W(vector, *q1_scr_free[0])
                    c0 = sDVE.inc(vector.tensor_scalar(
                        out=QSB, in0=wst(i), scalar1=thr1n, scalar2=thr1p,
                        op0=ALU.max, op1=ALU.min), 1)
                    W(vector, sDVE, c0)
                    cc = sDVE.inc(vector.tensor_tensor(
                        out=QSB, in0=wst(i), in1=QSB, op=ALU.subtract), 1)
                    wst_free[i] = (sDVE, cc)
                    q1_pre[fc] = cc
                block.vector(fq1)

                def fq1s(scalar, fc=fc):
                    j = st8["wbf"]
                    st8["wbf"] = (j + 1) % 3
                    s, n = wbf_free[j]
                    W(scalar, s, n)
                    W(scalar, sDVE, q1_pre[fc])
                    cc = sACT.inc(scalar.activation(
                        out=wbf(j), in_=QSB, func=AF.Sign), 1)
                    q1_scr_free[0] = (sACT, cc)
                    w1q_ready[fc] = (j, sACT, cc)
                block.scalar(fq1s)

                def ff1(tensor, fc=fc, gi=gi):
                    j, s, c = w1q_ready[fc]
                    W(tensor, s, c)
                    pb = fc % 2
                    W(tensor, *f_ps01_free[pb])
                    i = None
                    for lt in range(LT):
                        for dc in range(DC):
                            i = tensor.matmul(
                                PS[pb][:, lt * 512:lt * 512 + 512],
                                wbf(j)[:, dc * 128:dc * 128 + 128],
                                XnT[:, dc * T + lt * 512:dc * T + lt * 512 + 512],
                                start=(dc == 0), stop=(dc == DC - 1))
                    hstop = sPE.inc(i, 1)
                    wbf_free[j] = (sPE, hstop)
                    hT_ready[fc] = ("pe", hstop)
                    f_ps01_free[pb] = (sACT, 0)
                block.tensor(ff1)

                def fge(scalar, fc=fc, gi=gi):
                    _, hstop = hT_ready[fc]
                    W(scalar, sPE, hstop)
                    hs = fc % 8
                    W(scalar, *hT_free[hs])
                    c = sACT.inc(scalar.activation(
                        out=AB[:, hs * 1024:hs * 1024 + 1024],
                        in_=PS[fc % 2][:, :], func=AF.Gelu,
                        bias=MISC[:, B1C + fc:B1C + fc + 1]), 1)
                    hT_ready[fc] = (hs, c)
                    f_ps01_free[fc % 2] = (sACT, c)
                block.scalar(fge)
            f_fill_done[0] = sPE.n

            for l in ([] if _NODRAIN else range(LC)):
                stops = {}

                def fd2(tensor, gi=gi, grp=grp, l=l, stops=stops):
                    if l == 0:
                        if gi == 0:
                            W(tensor, sB2, f_b2)
                        for fc in grp:
                            hs, c = hT_ready[fc]
                            W(tensor, sACT, c)
                            _, s2, c2 = w2q_ready[fc]
                            W(tensor, s2, c2)
                    for dh in range(2):
                        pb = 2 + dh
                        W(tensor, *f_ps23_free[dh])
                        i = None
                        if gi == 0:
                            for db in range(2):
                                tensor.matmul(
                                    PS[pb][:, db * 512:db * 512 + 512], onesrow,
                                    ROWS[0:1, dh * 1024 + db * 512:dh * 1024 + db * 512 + 512],
                                    start=True, stop=False)
                        for k, fc in enumerate(grp):
                            hs, _ = hT_ready[fc]
                            os_, _, _ = w2q_ready[fc]
                            for db in range(2):
                                i = tensor.matmul(
                                    PS[pb][:, db * 512:db * 512 + 512],
                                    AB[:, hs * 1024 + l * 128:hs * 1024 + l * 128 + 128],
                                    OUTT[:, os_ * 2048 + dh * 1024 + db * 512:
                                         os_ * 2048 + dh * 1024 + db * 512 + 512],
                                    start=(gi > 0 and k == 0),
                                    stop=(k == len(grp) - 1))
                        stops[dh] = sPE.inc(i, 1)
                        f_ps23_free[dh] = (sDVE, 0)
                block.tensor(fd2)

                def fsp(vector, gi=gi, l=l, stops=stops):
                    if gi == 0 and l == 0:
                        W(vector, semST[0], e_st_final[0])
                        W(vector, semST[1], e_st_final[1])
                    for dh in range(2):
                        W(vector, sPE, stops[dh])
                        dst = GA[:, l * 2048 + dh * 1024:l * 2048 + dh * 1024 + 1024]
                        if gi == 0:
                            i = vector.tensor_copy(out=dst, in_=PS[2 + dh][:, :])
                        else:
                            i = vector.tensor_tensor(out=dst, in0=dst,
                                                     in1=PS[2 + dh][:, :],
                                                     op=ALU.add)
                        c = sDVE.inc(i, 1)
                        f_ps23_free[dh] = (sDVE, c)
                        spill_dve[(gi, l, dh)] = c
                block.vector(fsp)

            pe_now = sPE.n
            for fc in (grp if not _NODRAIN else []):
                hs, _ = hT_ready[fc]
                os_, _, _ = w2q_ready[fc]
                hT_free[hs] = (sPE, pe_now)
                w2q_free[os_] = (sPE, pe_now)

        gi_last = len(_FG) - 1

        if stop_after == "F":
            return _finish(nc, block, ALLSEMS, tap, locals())

        XS = XnT[:, :].bitcast(F32)
        prev_add = [0]
        prev_out = [0]
        g_sil = {}
        g_ld = {}
        for l in range(LC):
            def fg1(scalar, l=l):
                if l == 0:
                    W(scalar, sPE, f_fill_done[0])
                W(scalar, sSCD, sSCD.n)
                W(scalar, sDVE, prev_add[0])
                sSCD.inc(scalar.dma_start(
                    out=XS[:, 0:2048], in_=d_xtm[:, l * 2048:l * 2048 + 2048]), 16)
                sSCD.inc(scalar.dma_start(
                    out=XS[:, 2048:4096], in_=d_s[l * 128:l * 128 + 128, :]), 16)
                W(scalar, sSCD, sSCD.n)
                W(scalar, sDVE, spill_dve[(gi_last, l, 1)])
                c3 = sACT.inc(scalar.activation(
                    out=XS[:, 4096:6144], in_=GA[:, l * 2048:l * 2048 + 2048],
                    func=AF.Silu), 1)
                g_sil[l] = c3
            block.scalar(fg1)

            def fg2(vector, l=l):
                W(vector, sACT, g_sil[l])
                c = sDVE.inc(vector.tensor_tensor(
                    out=XS[:, 6144:8192], in0=XS[:, 2048:4096],
                    in1=XS[:, 4096:6144], op=ALU.mult), 1)
                W(vector, sDVE, c)
                prev_add[0] = sDVE.inc(vector.tensor_tensor(
                    out=XS[:, 6144:8192], in0=XS[:, 6144:8192],
                    in1=XS[:, 0:2048], op=ALU.add), 1)
            block.vector(fg2)

            def fg3(scalar, l=l):
                W(scalar, sDVE, prev_add[0])
                prev_out[0] = sSCD.inc(scalar.dma_start(
                    out=d_out[:, l * 2048:l * 2048 + 2048],
                    in_=XS[:, 6144:8192]), 16)
            block.scalar(fg3)

        return _finish(nc, block, ALLSEMS, tap, locals())


def _finish(nc, block, allsems, tap, env):
    sDMA, sSCD = env["sDMA"], env["sSCD"]
    sDVE = env["sDVE"]

    if tap is not None:
        d_dbg = env["d_dbg"]
        WST = env["WST"]

        def ftap_pre(sync):
            for s in allsems:
                if s.n > 0:
                    sync.wait_ge(s.h, s.n)
        block.sync(ftap_pre)

        if tap == "s_spill":
            d_s = env["d_s"]

            def ftapd(sync):
                sDMA.inc(sync.dma_start(
                    out=d_dbg.rearrange("p (a d) -> (p a) d", a=8, d=2048),
                    in_=d_s[:, :]), 16)
            block.sync(ftapd)
            src = None
        else:
            src = env[tap]

        if src is None:
            pass
        elif src.dtype == F32:
            def ftapf(sync):
                n = src.shape[1]
                sDMA.inc(sync.dma_start(out=d_dbg[:, 0:n], in_=src[:, :]), 16)
            block.sync(ftapf)
        else:
            n = src.shape[1]
            for k in range(0, n, 2048):
                w = min(2048, n - k)

                def fcv(vector, k=k, w=w):
                    if k == 0:
                        for s in allsems:
                            if s.n > 0 and s is not sDVE:
                                vector.wait_ge(s.h, s.n)
                    else:
                        vector.wait_ge(sDMA.h, sDMA.n)
                    sDVE.inc(vector.tensor_copy(out=WST[:, 0:w],
                                                in_=src[:, k:k + w]), 1)
                block.vector(fcv)

                def fcd(sync, k=k, w=w):
                    sync.wait_ge(sDVE.h, sDVE.n)
                    sDMA.inc(sync.dma_start(out=d_dbg[:, k:k + w],
                                            in_=WST[:, 0:w]), 16)
                block.sync(fcd)

    def ffin(sync):
        for s in allsems:
            if s.n > 0:
                sync.wait_ge(s.h, s.n)
    block.sync(ffin)
    return nc


def _host_layouts(inputs):
    x = np.asarray(inputs["L0_attn_x"])
    wq = np.asarray(inputs["L0_attn_wq"])
    wk = np.asarray(inputs["L0_attn_wk"])
    wv = np.asarray(inputs["L0_attn_wv"])
    wu = np.asarray(inputs["L0_attn_wu"])
    wo = np.asarray(inputs["L0_attn_wo"])
    w1 = np.asarray(inputs["L0_ffn_w1"])
    b1 = np.asarray(inputs["L0_ffn_b1"])
    w2 = np.asarray(inputs["L0_ffn_w2"])
    b2 = np.asarray(inputs["L0_ffn_b2"])

    def rhs_layout(w):
        return np.ascontiguousarray(
            w.reshape(EB, 512, DC, P).transpose(3, 0, 2, 1).reshape(P, EB * DC * 512))

    def lhs_layout(w, nec):
        return np.ascontiguousarray(
            w.reshape(nec, 128, DC, P).transpose(3, 0, 2, 1).reshape(P, nec * DC * 128))

    def w2_layout(w):
        return np.ascontiguousarray(
            w.reshape(D, FC, P).transpose(2, 1, 0).reshape(P, FC * D))

    consts = np.zeros((P, 8), dtype=np.float32)
    consts[:, 0] = LN_EPS
    consts[:, 1] = SRMS_EXPB
    consts[:, 2] = SRMS_GUARD

    w1l = lhs_layout(w1, FC)
    w2r = w2_layout(w2)
    common = dict(
        wk_r=rhs_layout(wk), wv_r=rhs_layout(wv), wo_r=rhs_layout(wo),
        wq_l=lhs_layout(wq, H), wu_l=lhs_layout(wu, H),
        w1_l=w1l, w2_r=w2r,
        b1r=np.ascontiguousarray(b1.reshape(FC, P).T),
        b2row=np.ascontiguousarray(b2.reshape(1, D)),
        consts=consts,
    )

    in_maps = []
    for c in range(NCORE):
        xc = x[0, c * T:(c + 1) * T, :]
        xT = np.ascontiguousarray(
            xc.reshape(T, DC, P).transpose(2, 1, 0).reshape(P, DC * T))
        xtm = np.ascontiguousarray(
            xc.reshape(LC, P, D).transpose(1, 0, 2).reshape(P, LC * D))
        m = dict(common)
        m["xT"] = xT
        m["xtm"] = xtm
        m["w1slice"] = np.ascontiguousarray(w1l[:, c * 16384:(c + 1) * 16384])
        m["w2slice"] = np.ascontiguousarray(w2r[:, c * 16384:(c + 1) * 16384])
        in_maps.append(m)
    return in_maps


def kernel(**inputs):
    in_maps = _host_layouts(inputs)
    nc = build_full()
    res = run_bass_kernel_spmd(nc, in_maps, core_ids=list(range(NCORE)))
    outs = []
    for c in range(NCORE):
        o = np.asarray(res.results[c]["out"])
        outs.append(o.reshape(P, LC, D).transpose(1, 0, 2).reshape(T, D))
    return np.concatenate(outs, axis=0).reshape(1, 8192, D)


# version 04 (final)
# speedup vs baseline: 2.1420x; 1.0686x over previous
import math
import os
import numpy as np
from contextlib import ExitStack

import concourse.bass as bass
import concourse.mybir as mybir
from concourse.bass_utils import run_bass_kernel_spmd

F32 = mybir.dt.float32
BF16 = mybir.dt.bfloat16
AF = mybir.ActivationFunctionType
ALU = mybir.AluOpType
AX = mybir.AxisListType

P = 128
T = 1024
LC = 8
LT = 2
D = 2048
DC = 16
FF = 8192
FC = 64
H = 16
EB = 4
NCORE = 8

INV_SCALE = 1.0 / math.sqrt(128.0)
SRMS_EXPB = 3.5 * math.log(2.0)
LN_EPS = 1e-5
SRMS_GUARD = 1e-30
INV_CLAMP = 1e12
THR_SCALE = 0.7 / float(8192 * 2048)

GSIZE = 6
FC_GROUPS = [list(range(g, min(g + GSIZE, FC))) for g in range(0, FC, GSIZE)]

B1C = 0
THRC = 64
WPC = 68
WPT = 70
OCC = 72
CSTC = 72
ONES = 80
THS = 208


class Sem:
    __slots__ = ("h", "n")

    def __init__(self, handle):
        self.h = handle
        self.n = 0

    def inc(self, instr, k=1):
        instr.then_inc(self.h, k)
        self.n += k
        return self.n


def build_full(stop_after=None, tap=None):
    nc = bass.Bass("TRN2", target_bir_lowering=False)

    d_xT = nc.dram_tensor("xT", [P, DC * T], F32, kind="ExternalInput").ap()
    d_xtm = nc.dram_tensor("xtm", [P, LC * D], F32, kind="ExternalInput").ap()
    d_wk = nc.dram_tensor("wk_r", [P, EB * DC * 512], F32, kind="ExternalInput").ap()
    d_wv = nc.dram_tensor("wv_r", [P, EB * DC * 512], F32, kind="ExternalInput").ap()
    d_wo = nc.dram_tensor("wo_r", [P, EB * DC * 512], F32, kind="ExternalInput").ap()
    d_wq = nc.dram_tensor("wq_l", [P, H * DC * 128], F32, kind="ExternalInput").ap()
    d_wu = nc.dram_tensor("wu_l", [P, H * DC * 128], F32, kind="ExternalInput").ap()
    d_w1 = nc.dram_tensor("w1_l", [P, FC * DC * 128], F32, kind="ExternalInput").ap()
    d_w2 = nc.dram_tensor("w2_r", [P, FC * D], F32, kind="ExternalInput").ap()
    d_w1s = nc.dram_tensor("w1slice", [P, 8 * 2048], F32, kind="ExternalInput").ap()
    d_w2s = nc.dram_tensor("w2slice", [P, 8 * 2048], F32, kind="ExternalInput").ap()
    d_b1 = nc.dram_tensor("b1r", [P, FC], F32, kind="ExternalInput").ap()
    d_b2 = nc.dram_tensor("b2row", [1, D], F32, kind="ExternalInput").ap()
    d_cst = nc.dram_tensor("consts", [P, 8], F32, kind="ExternalInput").ap()

    d_out = nc.dram_tensor("out", [P, LC * D], F32, kind="ExternalOutput").ap()
    d_dbg = None
    if tap is not None:
        d_dbg = nc.dram_tensor("dbg", [P, DC * T], F32, kind="ExternalOutput").ap()

    d_s = nc.dram_tensor("s_spill", [T, D], F32).ap()
    t_ccin = nc.dram_tensor("cc_in", [P + 1, D], F32)
    t_ccout = nc.dram_tensor("cc_out", [P + 1, D], F32)
    d_ccin = t_ccin.ap()
    d_ccout = t_ccout.ap()

    with ExitStack() as ctx:
        En = ctx.enter_context
        XnT = En(nc.sbuf_tensor("XnT", [P, DC * T], BF16))
        AB = En(nc.sbuf_tensor("AB", [P, DC * T], BF16))
        OUTT = En(nc.sbuf_tensor("OUTT", [P, DC * T], BF16))
        GA = En(nc.sbuf_tensor("GA", [P, DC * T], F32))
        WST = En(nc.sbuf_tensor("WST", [P, 2 * 2048], F32))
        WBF = En(nc.sbuf_tensor("WBF", [P, 3 * 2048], BF16))
        MISC = En(nc.sbuf_tensor("MISC", [P, 224], F32))
        MISCB = En(nc.sbuf_tensor("MISCB", [P, D], BF16))
        ROWS = En(nc.sbuf_tensor("ROWS", [P, 3072], F32))
        PS = [En(nc.psum_tensor(f"PS{i}", [P, 1024], F32)) for i in range(4)]

        sDMA = Sem(En(nc.semaphore("sDMA")))
        sSCD = Sem(En(nc.semaphore("sSCD")))
        sGD = Sem(En(nc.semaphore("sGD")))
        sPE = Sem(En(nc.semaphore("sPE")))
        sDVE = Sem(En(nc.semaphore("sDVE")))
        sACT = Sem(En(nc.semaphore("sACT")))
        sPOOL = Sem(En(nc.semaphore("sPOOL")))
        sCC = Sem(En(nc.semaphore("sCC")))
        sB2 = Sem(En(nc.semaphore("sB2")))
        sROW = Sem(En(nc.semaphore("sROW")))
        sTHR = Sem(En(nc.semaphore("sTHR")))
        sW0 = Sem(En(nc.semaphore("sW0")))
        sW1 = Sem(En(nc.semaphore("sW1")))
        semW = [sW0, sW1]
        sST0 = Sem(En(nc.semaphore("sST0")))
        sST1 = Sem(En(nc.semaphore("sST1")))
        semST = [sST0, sST1]
        ALLSEMS = [sDMA, sSCD, sGD, sPE, sDVE, sACT, sPOOL, sCC, sB2, sROW,
                   sTHR, sW0, sW1, sST0, sST1]

        block = En(nc.Block())

        def W(eng, sem, n):
            if n > 0:
                eng.wait_ge(sem.h, n)

        def wst(i):
            return WST[:, i * 2048:(i + 1) * 2048]

        def wbf(i):
            return WBF[:, i * 2048:(i + 1) * 2048]

        def bank(i):
            return PS[i // 2][:, (i % 2) * 512:(i % 2) * 512 + 512]

        onesrow = MISC[0:1, ONES:ONES + 128]
        onesmat = MISC[:, ONES:ONES + 128]
        eps_ap = MISC[:, CSTC + 0:CSTC + 1]
        expb_ap = MISC[:, CSTC + 1:CSTC + 2]
        guard_ap = MISC[:, CSTC + 2:CSTC + 3]

        wst_free = [(sDVE, 0), (sDVE, 0)]
        wbf_free = [(sPE, 0), (sPE, 0), (sPE, 0)]
        st8 = {"wst": 1, "wbf": 0}
        wst_ready = {}
        wbf_ready = {}

        def wload(src_ap, jid):
            i = st8["wst"]
            st8["wst"] = 1 - i

            def f(sync, i=i, src_ap=src_ap, jid=jid):
                s, n = wst_free[i]
                W(sync, s, n)
                c = semW[i].inc(sync.dma_start(out=wst(i), in_=src_ap), 16)
                wst_ready[jid] = (i, c)
            block.sync(f)

        def wcast(jid):
            j = st8["wbf"]
            st8["wbf"] = (j + 1) % 3

            def f(scalar, j=j, jid=jid):
                i, c = wst_ready[jid]
                s, n = wbf_free[j]
                W(scalar, s, n)
                W(scalar, semW[i], c)
                cc = sACT.inc(scalar.copy(out=wbf(j), in_=wst(i)), 1)
                wst_free[i] = (sACT, cc)
                wbf_ready[jid] = (j, sACT, cc)
            block.scalar(f)

        def fA_loads(sync):
            sDMA.inc(sync.dma_start(out=GA[:, 0:8192], in_=d_xT[:, 0:8192]), 16)
            sDMA.inc(sync.dma_start(out=GA[:, 8192:16384], in_=d_xT[:, 8192:16384]), 16)
            sDMA.inc(sync.dma_start(out=MISC[:, CSTC:CSTC + 8], in_=d_cst[:, :]), 16)
            sDMA.inc(sync.dma_start(out=MISC[:, B1C:B1C + FC], in_=d_b1[:, :]), 16)
        block.sync(fA_loads)
        aload = sDMA.n

        def fA_memset(vector):
            sDVE.inc(vector.memset(MISC[:, ONES:ONES + 128], 1.0), 1)
        block.vector(fA_memset)
        v_ones = sDVE.n

        sumsq_mm = []
        sq_cnt = []
        for dc in range(DC):
            def fsq(scalar, dc=dc):
                if dc == 0:
                    W(scalar, sDMA, aload)
                if dc >= 2:
                    W(scalar, sPE, sumsq_mm[dc - 2])
                c = sACT.inc(scalar.activation(
                    out=ROWS[:, (dc % 2) * 1024:(dc % 2) * 1024 + 1024],
                    in_=GA[:, dc * T:dc * T + 1024], func=AF.Square), 1)
                sq_cnt.append(c)
            block.scalar(fsq)

            def fmm(tensor, dc=dc):
                if dc == 0:
                    W(tensor, sDMA, aload)
                    W(tensor, sDVE, v_ones)
                for lt in range(LT):
                    tensor.matmul(bank(lt), onesmat,
                                  GA[:, dc * T + lt * 512: dc * T + lt * 512 + 512],
                                  start=(dc == 0), stop=(dc == DC - 1))
                W(tensor, sACT, sq_cnt[dc])
                i = None
                for lt in range(LT):
                    i = tensor.matmul(
                        bank(2 + lt), onesmat,
                        ROWS[:, (dc % 2) * 1024 + lt * 512:(dc % 2) * 1024 + lt * 512 + 512],
                        start=(dc == 0), stop=(dc == DC - 1))
                sumsq_mm.append(sPE.inc(i, 1))
            block.tensor(fmm)
        a_stats_mm = sPE.n

        MB = ROWS[:, 0:1024]
        EXB = ROWS[:, 1024:2048]
        SCB = ROWS[:, 2048:3072]

        def fA_evac(scalar):
            W(scalar, sPE, a_stats_mm)
            for lt in range(LT):
                scalar.activation(out=MB[:, lt * 512:lt * 512 + 512],
                                  in_=bank(lt), func=AF.Copy, scale=1.0 / D)
            for lt in range(LT):
                sACT.inc(scalar.activation(out=EXB[:, lt * 512:lt * 512 + 512],
                                           in_=bank(2 + lt), func=AF.Copy,
                                           scale=1.0 / D), 1)
        block.scalar(fA_evac)
        a_mrow = sACT.n

        def fA_var(vector):
            W(vector, sACT, a_mrow)
            c = sDVE.inc(vector.tensor_tensor(out=SCB, in0=MB, in1=MB,
                                              op=ALU.mult), 1)
            W(vector, sDVE, c)
            sDVE.inc(vector.tensor_tensor(out=SCB, in0=EXB, in1=SCB,
                                          op=ALU.subtract), 1)
        block.vector(fA_var)
        a_var = sDVE.n

        def fA_rstd(scalar):
            W(scalar, sDVE, a_var)
            scalar.activation(out=EXB, in_=SCB, func=AF.Ln, bias=eps_ap)
            sACT.inc(scalar.activation(out=SCB, in_=EXB, func=AF.Exp,
                                       scale=-0.5), 1)
        block.scalar(fA_rstd)
        a_rstd = sACT.n

        apply_cnt = []
        def fA_apply(vector):
            W(vector, sACT, a_rstd)
            for dc in range(DC):
                tmp = WST[:, (dc % 2) * 1024:(dc % 2) * 1024 + 1024]
                c = sDVE.inc(vector.tensor_tensor(
                    out=tmp, in0=GA[:, dc * T:dc * T + 1024],
                    in1=MB, op=ALU.subtract), 1)
                W(vector, sDVE, c)
                apply_cnt.append(sDVE.inc(vector.tensor_tensor(
                    out=XnT[:, dc * T:dc * T + 1024], in0=tmp, in1=SCB,
                    op=ALU.mult), 1))
        block.vector(fA_apply)
        a_apply = sDVE.n
        wst_free[0] = (sDVE, a_apply)
        wst_free[1] = (sDVE, 0)

        if stop_after == "A":
            return _finish(nc, block, ALLSEMS, tap, locals())

        bank_free = [(sACT, a_mrow)] * 4 + [(sDVE, 0)] * 4
        kv_evac = []
        stats_k = 0

        for eb in range(EB):
            for wname, dap in (("k", d_wk), ("v", d_wv)):
                evac_ready = {}
                last_slot = [0]
                for cg in range(4):
                    j = f"B{wname}{eb}_{cg}"
                    off = (eb * DC + cg * 4) * 512
                    wload(dap[:, off:off + 2048], j)
                    wcast(j)

                    def fchunk(tensor, eb=eb, wname=wname, cg=cg, j=j,
                               evac_ready=evac_ready):
                        if eb == 0 and wname == "k" and cg == 0:
                            W(tensor, sDVE, a_apply)
                        slot, s, c = wbf_ready[j]
                        W(tensor, s, c)
                        if cg == 0:
                            for l in range(8):
                                W(tensor, *bank_free[l])
                        for dc in range(cg * 4, cg * 4 + 4):
                            for l in range(LC):
                                i = tensor.matmul(
                                    bank(l),
                                    XnT[:, dc * T + l * 128: dc * T + l * 128 + 128],
                                    wbf(slot)[:, (dc % 4) * 512:(dc % 4) * 512 + 512],
                                    start=(dc == 0), stop=(dc == DC - 1))
                                if dc == DC - 1:
                                    evac_ready[l] = sPE.inc(i, 1)
                                elif dc % 4 == 3 and l == LC - 1:
                                    wbf_free[slot] = (sPE, sPE.inc(i, 1))
                        last_slot[0] = slot
                    block.tensor(fchunk)
                wbf_free[last_slot[0]] = (sPE, evac_ready[LC - 1])

                def fevac(scalar, eb=eb, wname=wname, evac_ready=evac_ready):
                    base = 0 if wname == "k" else 4096
                    for l in range(LC):
                        W(scalar, sPE, evac_ready[l])
                        if wname == "k":
                            i = scalar.activation(
                                out=AB[:, base + l * 512:base + l * 512 + 512],
                                in_=bank(l), func=AF.Relu, scale=INV_SCALE)
                        else:
                            i = scalar.copy(
                                out=AB[:, base + l * 512:base + l * 512 + 512],
                                in_=bank(l))
                        bank_free[l] = (sACT, sACT.inc(i, 1))
                block.scalar(fevac)

            for st in ([] if os.environ.get("KDBG_NOSTATS") == "1" else range(4)):
                k = eb * 4 + st
                j = f"Bst{k}"
                src = d_w1s if k < 8 else d_w2s
                kk = k % 8
                wload(src[:, kk * 2048:kk * 2048 + 2048], j)

                def fstat(vector, j=j, k=k):
                    i, c = wst_ready[j]
                    W(vector, semW[i], c)
                    cc = sDVE.inc(vector.tensor_reduce(
                        out=MISC[:, THS + k:THS + k + 1], in_=wst(i),
                        axis=AX.X, op=ALU.add, apply_absolute_value=True), 1)
                    wst_free[i] = (sDVE, cc)
                block.vector(fstat)
                stats_k += 1

            def fkv(tensor, eb=eb):
                for l in range(LC):
                    W(tensor, *bank_free[l])
                last = None
                for hh in range(4):
                    for l in range(LC):
                        last = tensor.matmul(
                            bank(0)[:, hh * 128:hh * 128 + 128],
                            AB[:, l * 512 + hh * 128:l * 512 + hh * 128 + 128],
                            AB[:, 4096 + l * 512 + hh * 128:4096 + l * 512 + hh * 128 + 128],
                            start=(l == 0), stop=(l == LC - 1))
                sPE.inc(last, 1)
            block.tensor(fkv)
            kv_mm = sPE.n

            def fkve(scalar, eb=eb, kv_mm=kv_mm):
                W(scalar, sPE, kv_mm)
                kv_evac.append(sACT.inc(scalar.copy(
                    out=GA[:, eb * 512:eb * 512 + 512], in_=bank(0)), 1))
            block.scalar(fkve)
            bank_free[0] = (sACT, kv_evac[-1])

        b_stats_done = sDVE.n

        def fzrow(vector):
            sDVE.inc(vector.memset(GA[0:1, 2048:4096], 0.0), 1)
        block.vector(fzrow)
        b_zrow = sDVE.n

        def fwsum(tensor):
            W(tensor, sDVE, b_stats_done)
            W(tensor, *bank_free[2])
            sPE.inc(tensor.matmul(bank(2)[:, 0:16], onesmat,
                                  MISC[:, THS:THS + 16],
                                  start=True, stop=True), 1)
        block.tensor(fwsum)
        b_wsum_mm = sPE.n

        def fwev(scalar):
            W(scalar, sPE, b_wsum_mm)
            sACT.inc(scalar.copy(out=MISC[:, THS:THS + 16],
                                 in_=bank(2)[:, 0:16]), 1)
        block.scalar(fwev)
        b_wev = sACT.n
        bank_free[2] = (sACT, b_wev)

        def fwred(vector):
            W(vector, sACT, b_wev)
            vector.tensor_reduce(out=MISC[:, WPC:WPC + 1],
                                 in_=MISC[:, THS:THS + 8], axis=AX.X,
                                 op=ALU.add)
            sDVE.inc(vector.tensor_reduce(out=MISC[:, WPC + 1:WPC + 2],
                                          in_=MISC[:, THS + 8:THS + 16],
                                          axis=AX.X, op=ALU.add), 1)
        block.vector(fwred)
        b_wred = sDVE.n

        def fwstage(scalar):
            W(scalar, sDVE, b_wred)
            W(scalar, sDVE, b_zrow)
            sACT.inc(scalar.copy(out=GA[0:1, 2048:2050],
                                 in_=MISC[0:1, WPC:WPC + 2]), 1)
        block.scalar(fwstage)
        b_wsum = sACT.n

        def fcc_start(gpsimd):
            for c in kv_evac:
                W(gpsimd, sACT, c)
            W(gpsimd, sACT, b_wsum)
            sGD.inc(gpsimd.dma_start(out=d_ccin[0:P, :], in_=GA[:, 0:2048]), 16)
            sROW.inc(gpsimd.dma_start(out=d_ccin[P:P + 1, :],
                                      in_=GA[0:1, 2048:4096]), 16)
            W(gpsimd, sGD, sGD.n)
            W(gpsimd, sROW, sROW.n)
            i = gpsimd.collective_compute(
                "AllReduce", ALU.add, replica_groups=[list(range(NCORE))],
                ins=[t_ccin.ap().opt()], outs=[t_ccout.ap().opt()])
            sCC.inc(i, 1)
        block.gpsimd(fcc_start)

        if stop_after == "B":
            return _finish(nc, block, ALLSEMS, tap, locals())

        c_bankfree = [bank_free[0], bank_free[1], bank_free[2], bank_free[3]]
        q_done = {}
        for ec in range(H):
            j = f"Cq{ec}"
            wload(d_wq[:, ec * 2048:ec * 2048 + 2048], j)
            wcast(j)

            def fq(tensor, ec=ec, j=j):
                slot, s, c = wbf_ready[j]
                W(tensor, s, c)
                stops = []
                for lt in range(LT):
                    b = (ec * 2 + lt) % 4
                    W(tensor, *c_bankfree[b])
                    i = None
                    for dc in range(DC):
                        i = tensor.matmul(
                            bank(b), wbf(slot)[:, dc * 128:dc * 128 + 128],
                            XnT[:, dc * T + lt * 512:dc * T + lt * 512 + 512],
                            start=(dc == 0), stop=(dc == DC - 1))
                    stops.append(sPE.inc(i, 1))
                wbf_free[slot] = (sPE, stops[-1])
                q_done[("mm", ec)] = stops
            block.tensor(fq)

            def fqe(scalar, ec=ec):
                stops = q_done[("mm", ec)]
                for lt in range(LT):
                    W(scalar, sPE, stops[lt])
                    b = (ec * 2 + lt) % 4
                    c = sACT.inc(scalar.activation(
                        out=AB[:, ec * T + lt * 512:ec * T + lt * 512 + 512],
                        in_=bank(b), func=AF.Relu, scale=INV_SCALE), 1)
                    c_bankfree[b] = (sACT, c)
                q_done[ec] = c
            block.scalar(fqe)

        def fcc_fin(gpsimd):
            W(gpsimd, sCC, sCC.n)
            sGD.inc(gpsimd.dma_start(out=GA[:, 0:2048], in_=d_ccout[0:P, :]), 16)
            sTHR.inc(gpsimd.dma_start(out=MISC[0:1, WPT:WPT + 2],
                                      in_=d_ccout[P:P + 1, 0:2]), 16)
            W(gpsimd, sGD, sGD.n)
            sPOOL.inc(gpsimd.tensor_copy(out=MISCB[:, :], in_=GA[:, 0:2048]), 1)
        block.gpsimd(fcc_fin)
        b_kvg = sPOOL.n
        b_thr_ld = sTHR.n

        def fthr_rows(scalar):
            W(scalar, sTHR, b_thr_ld)
            scalar.activation(out=MISC[0:1, THS:THS + 2],
                              in_=MISC[0:1, WPT:WPT + 2], func=AF.Copy,
                              scale=THR_SCALE)
            sACT.inc(scalar.activation(out=MISC[0:1, THS + 2:THS + 4],
                                       in_=MISC[0:1, WPT:WPT + 2], func=AF.Copy,
                                       scale=-THR_SCALE), 1)
        block.scalar(fthr_rows)
        b_thr_rows = sACT.n

        def fthr_mm(tensor):
            W(tensor, sACT, b_thr_rows)
            W(tensor, *bank_free[4])
            W(tensor, *bank_free[5])
            sPE.inc(tensor.matmul(bank(4)[:, 0:4], onesrow,
                                  MISC[0:1, THS:THS + 4], start=True, stop=True), 1)
        block.tensor(fthr_mm)
        b_thr_mm = sPE.n

        def fthr_ev(scalar):
            W(scalar, sPE, b_thr_mm)
            sACT.inc(scalar.copy(out=MISC[:, THRC:THRC + 4],
                                 in_=bank(4)[:, 0:4]), 1)
        block.scalar(fthr_ev)
        b_thr = sACT.n

        if stop_after == "C":
            return _finish(nc, block, ALLSEMS, tap, locals())

        d_ps0_free = c_bankfree[0]
        d_ps0_free2 = c_bankfree[1]
        d_ps1_free = c_bankfree[2]
        d_ps1_free2 = c_bankfree[3]
        d_ps2_free = (sACT, b_thr)
        ga_gate = (sPOOL, b_kvg)
        for g in range(4):
            d1 = {}
            for hh in range(4):
                h = g * 4 + hh
                wload(d_wu[:, h * 2048:h * 2048 + 2048], f"Du{h}")
                wcast(f"Du{h}")

                def fd1(tensor, h=h, hh=hh, g=g):
                    slot, s, c = wbf_ready[f"Du{h}"]
                    W(tensor, s, c)
                    W(tensor, *d_ps0_free)
                    W(tensor, *d_ps0_free2)
                    i = None
                    for lt in range(LT):
                        for dc in range(DC):
                            i = tensor.matmul(
                                PS[0][:, lt * 512:lt * 512 + 512],
                                wbf(slot)[:, dc * 128:dc * 128 + 128],
                                XnT[:, dc * T + lt * 512:dc * T + lt * 512 + 512],
                                start=(dc == 0), stop=(dc == DC - 1))
                    d1[("u", hh)] = sPE.inc(i, 1)
                    wbf_free[slot] = (sPE, d1[("u", hh)])
                    if g == 0 and hh == 0:
                        W(tensor, sPOOL, b_kvg)
                    W(tensor, sACT, q_done[h])
                    W(tensor, *d_ps1_free)
                    W(tensor, *d_ps1_free2)
                    for lt in range(LT):
                        i = tensor.matmul(
                            PS[1][:, lt * 512:lt * 512 + 512],
                            MISCB[:, h * 128:h * 128 + 128],
                            AB[:, h * T + lt * 512:h * T + lt * 512 + 512],
                            start=True, stop=True)
                    d1[("o", hh)] = sPE.inc(i, 1)
                block.tensor(fd1)

                def fd1e(scalar, h=h, hh=hh, g=g):
                    W(scalar, sPE, d1[("u", hh)])
                    if g == 0:
                        W(scalar, *ga_gate)
                    cu = sACT.inc(scalar.activation(
                        out=GA[:, hh * 1024:hh * 1024 + 1024], in_=PS[0][:, :],
                        func=AF.Silu), 1)
                    W(scalar, sPE, d1[("o", hh)])
                    co = sACT.inc(scalar.copy(
                        out=GA[:, 4096 + hh * 1024:4096 + hh * 1024 + 1024],
                        in_=PS[1][:, :]), 1)
                    d1[("ue", hh)] = cu
                    d1[("oe", hh)] = co
                block.scalar(fd1e)
                d_ps0_free = (sACT, sACT.n - 1)
                d_ps0_free2 = (sACT, 0)
                d_ps1_free = (sACT, sACT.n)
                d_ps1_free2 = (sACT, 0)

            for hh in range(4):
                h = g * 4 + hh
                Uh = GA[:, hh * 1024:hh * 1024 + 1024]
                Oh = GA[:, 4096 + hh * 1024:4096 + hh * 1024 + 1024]
                OSQ = GA[:, 8192:9216]
                INV = GA[:, 9216:10240]

                def fsq2(vector, hh=hh, Oh=Oh, OSQ=OSQ):
                    W(vector, sACT, d1[("oe", hh)])
                    if hh == 0:
                        W(vector, sDVE, sDVE.n)
                    sDVE.inc(vector.tensor_tensor(out=OSQ, in0=Oh, in1=Oh,
                                                  op=ALU.mult), 1)
                block.vector(fsq2)
                d_osq = sDVE.n

                def fss(tensor, d_osq=d_osq, OSQ=OSQ):
                    W(tensor, sDVE, d_osq)
                    W(tensor, *d_ps2_free)
                    i = None
                    for lt in range(LT):
                        i = tensor.matmul(PS[2][:, lt * 512:lt * 512 + 512],
                                          onesmat, OSQ[:, lt * 512:lt * 512 + 512],
                                          start=True, stop=True)
                    sPE.inc(i, 1)
                block.tensor(fss)
                d_ss = sPE.n

                def finv(scalar, d_ss=d_ss, INV=INV):
                    W(scalar, sPE, d_ss)
                    sACT.inc(scalar.activation(out=INV, in_=PS[2][:, :],
                                               func=AF.Ln, bias=guard_ap), 1)
                    sACT.inc(scalar.activation(out=INV, in_=INV, func=AF.Exp,
                                               scale=-0.5, bias=expb_ap), 1)
                block.scalar(finv)
                d_ps2_free = (sACT, sACT.n - 1)
                d_inv = sACT.n

                def fgate(vector, h=h, d_inv=d_inv, Uh=Uh, Oh=Oh, OSQ=OSQ, INV=INV):
                    W(vector, sACT, d_inv)
                    c = sDVE.inc(vector.tensor_scalar_min(out=INV, in0=INV,
                                                          scalar1=INV_CLAMP), 1)
                    W(vector, sDVE, c)
                    c = sDVE.inc(vector.tensor_tensor(out=OSQ, in0=Oh, in1=INV,
                                                      op=ALU.mult), 1)
                    W(vector, sDVE, c)
                    sDVE.inc(vector.tensor_tensor(
                        out=OUTT[:, h * T:h * T + 1024], in0=OSQ, in1=Uh,
                        op=ALU.mult), 1)
                block.vector(fgate)
        d_done = sDVE.n
        d_qt_done = sPE.n

        if stop_after == "D":
            return _finish(nc, block, ALLSEMS, tap, locals())

        e_bankfree = [d_ps0_free, d_ps0_free, d_ps1_free, d_ps1_free,
                      d_ps2_free, d_ps2_free, bank_free[6], bank_free[7]]
        def ebank(l):
            return bank(l)
        e_stage_free = [(sSCD, 0), (sSCD, 0)]
        e_sdma_last = 0

        for eb in range(EB):
            evac_ready = {}
            last_slot = [0]
            for cg in range(4):
                j = f"Eo{eb}_{cg}"
                off = (eb * DC + cg * 4) * 512
                wload(d_wo[:, off:off + 2048], j)
                wcast(j)

                def fe(tensor, eb=eb, cg=cg, j=j, evac_ready=evac_ready):
                    if eb == 0 and cg == 0:
                        W(tensor, sDVE, d_done)
                    slot, s, c = wbf_ready[j]
                    W(tensor, s, c)
                    if cg == 0:
                        for l in range(8):
                            W(tensor, *e_bankfree[l])
                    for dc in range(cg * 4, cg * 4 + 4):
                        for l in range(LC):
                            i = tensor.matmul(
                                bank(l), OUTT[:, dc * T + l * 128:dc * T + l * 128 + 128],
                                wbf(slot)[:, (dc % 4) * 512:(dc % 4) * 512 + 512],
                                start=(dc == 0), stop=(dc == DC - 1))
                            if dc == DC - 1:
                                evac_ready[l] = sPE.inc(i, 1)
                            elif dc % 4 == 3 and l == LC - 1:
                                wbf_free[slot] = (sPE, sPE.inc(i, 1))
                    last_slot[0] = slot
                block.tensor(fe)
            wbf_free[last_slot[0]] = (sPE, evac_ready[LC - 1])

            def fes(scalar, eb=eb, evac_ready=evac_ready):
                W(scalar, sPE, evac_ready[LC - 1])
                W(scalar, *e_stage_free[0])
                for l in range(LC):
                    stage = GA[:, l * 512:l * 512 + 512]
                    e_bankfree[l] = (sACT, sACT.inc(
                        scalar.copy(out=stage, in_=ebank(l)), 1))
                cd = 0
                for l in range(LC):
                    stage = GA[:, l * 512:l * 512 + 512]
                    cd = semST[0].inc(scalar.dma_start(
                        out=d_s[l * 128:l * 128 + 128, eb * 512:eb * 512 + 512],
                        in_=stage), 16)
                e_stage_free[0] = (semST[0], cd)
            block.scalar(fes)
        e_sdma_last = 0
        e_st_final = (semST[0].n, semST[1].n)

        if stop_after == "E":
            return _finish(nc, block, ALLSEMS, tap, locals())

        QSA = AB[:, 8192:12288].bitcast(F32)
        QSB = AB[:, 12288:16384].bitcast(F32)
        thr1p = MISC[:, THRC + 0:THRC + 1]
        thr2p = MISC[:, THRC + 1:THRC + 2]
        thr1n = MISC[:, THRC + 2:THRC + 3]
        thr2n = MISC[:, THRC + 3:THRC + 4]

        e_pe_done = sPE.n
        w2q_free = [(sPE, e_pe_done)] * 8
        hT_free = [(sPE, d_qt_done)] * 8
        w2q_ready = {}
        hT_ready = {}
        w1q_ready = {}
        f_ps01_free = [(sACT, max(e_bankfree[0][1], e_bankfree[1][1])),
                       (sACT, max(e_bankfree[2][1], e_bankfree[3][1]))]
        f_ps23_free = [(sACT, max(e_bankfree[4][1], e_bankfree[5][1])),
                       (sACT, max(e_bankfree[6][1], e_bankfree[7][1]))]
        spill_dve = {}

        if os.environ.get("KDBG_NOB2") == "1":
            f_b2 = 0
        else:
            def fb2(scalar):
                W(scalar, sDVE, a_apply)
                sB2.inc(scalar.dma_start(out=ROWS[0:1, 0:2048],
                                         in_=d_b2[0:1, :]), 16)
            block.scalar(fb2)
            f_b2 = sB2.n

        first_quant = [True, True]
        q1_pre = {}
        q2_pre = {}
        q1_scr_free = [(sACT, 0), (sACT, 0)]
        q2_scr_free = [(sACT, 0), (sACT, 0)]
        f_fill_done = [0]
        _FG = FC_GROUPS
        if os.environ.get("KDBG_FGRP"):
            _FG = FC_GROUPS[:int(os.environ["KDBG_FGRP"])]
        _NODRAIN = os.environ.get("KDBG_NODRAIN") == "1"
        for gi, grp in enumerate(_FG):
            for fc in grp:
                wload(d_w2[:, fc * 2048:fc * 2048 + 2048], f"F2_{fc}")

                def fq2(vector, fc=fc):
                    i, c = wst_ready[f"F2_{fc}"]
                    W(vector, semW[i], c)
                    if first_quant[0]:
                        W(vector, sACT, b_thr)
                        W(vector, sPE, d_qt_done)
                        first_quant[0] = False
                    for hb in range(2):
                        scr = QSA[:, hb * 1024:hb * 1024 + 1024]
                        win = wst(i)[:, hb * 1024:hb * 1024 + 1024]
                        W(vector, *q2_scr_free[hb])
                        c0 = sDVE.inc(vector.tensor_scalar(
                            out=scr, in0=win, scalar1=thr2n, scalar2=thr2p,
                            op0=ALU.max, op1=ALU.min), 1)
                        W(vector, sDVE, c0)
                        cc = sDVE.inc(vector.tensor_tensor(
                            out=scr, in0=win, in1=scr, op=ALU.subtract), 1)
                        q2_pre[(fc, hb)] = cc
                    wst_free[i] = (sDVE, cc)
                block.vector(fq2)

                def fq2s(scalar, fc=fc):
                    oslot = fc % 8
                    W(scalar, *w2q_free[oslot])
                    for hb in range(2):
                        scr = QSA[:, hb * 1024:hb * 1024 + 1024]
                        W(scalar, sDVE, q2_pre[(fc, hb)])
                        cc = sACT.inc(scalar.activation(
                            out=OUTT[:, oslot * 2048 + hb * 1024:
                                     oslot * 2048 + hb * 1024 + 1024],
                            in_=scr, func=AF.Sign), 1)
                        q2_scr_free[hb] = (sACT, cc)
                    w2q_ready[fc] = (oslot, sACT, cc)
                block.scalar(fq2s)

                wload(d_w1[:, fc * 2048:fc * 2048 + 2048], f"F1_{fc}")

                def fq1(vector, fc=fc):
                    i, c = wst_ready[f"F1_{fc}"]
                    W(vector, semW[i], c)
                    if first_quant[1]:
                        W(vector, sACT, b_thr)
                        W(vector, sPE, d_qt_done)
                        first_quant[1] = False
                    for hb in range(2):
                        scr = QSB[:, hb * 1024:hb * 1024 + 1024]
                        win = wst(i)[:, hb * 1024:hb * 1024 + 1024]
                        W(vector, *q1_scr_free[hb])
                        c0 = sDVE.inc(vector.tensor_scalar(
                            out=scr, in0=win, scalar1=thr1n, scalar2=thr1p,
                            op0=ALU.max, op1=ALU.min), 1)
                        W(vector, sDVE, c0)
                        cc = sDVE.inc(vector.tensor_tensor(
                            out=scr, in0=win, in1=scr, op=ALU.subtract), 1)
                        q1_pre[(fc, hb)] = cc
                    wst_free[i] = (sDVE, cc)
                block.vector(fq1)

                def fq1s(scalar, fc=fc):
                    j = st8["wbf"]
                    st8["wbf"] = (j + 1) % 3
                    s, n = wbf_free[j]
                    W(scalar, s, n)
                    for hb in range(2):
                        scr = QSB[:, hb * 1024:hb * 1024 + 1024]
                        W(scalar, sDVE, q1_pre[(fc, hb)])
                        cc = sACT.inc(scalar.activation(
                            out=wbf(j)[:, hb * 1024:hb * 1024 + 1024],
                            in_=scr, func=AF.Sign), 1)
                        q1_scr_free[hb] = (sACT, cc)
                    w1q_ready[fc] = (j, sACT, cc)
                block.scalar(fq1s)

                def ff1(tensor, fc=fc, gi=gi):
                    j, s, c = w1q_ready[fc]
                    W(tensor, s, c)
                    pb = fc % 2
                    W(tensor, *f_ps01_free[pb])
                    i = None
                    for lt in range(LT):
                        for dc in range(DC):
                            i = tensor.matmul(
                                PS[pb][:, lt * 512:lt * 512 + 512],
                                wbf(j)[:, dc * 128:dc * 128 + 128],
                                XnT[:, dc * T + lt * 512:dc * T + lt * 512 + 512],
                                start=(dc == 0), stop=(dc == DC - 1))
                    hstop = sPE.inc(i, 1)
                    wbf_free[j] = (sPE, hstop)
                    hT_ready[fc] = ("pe", hstop)
                    f_ps01_free[pb] = (sACT, 0)
                block.tensor(ff1)

                def fge(scalar, fc=fc, gi=gi):
                    _, hstop = hT_ready[fc]
                    W(scalar, sPE, hstop)
                    hs = fc % 8
                    W(scalar, *hT_free[hs])
                    c = sACT.inc(scalar.activation(
                        out=AB[:, hs * 1024:hs * 1024 + 1024],
                        in_=PS[fc % 2][:, :], func=AF.Gelu,
                        bias=MISC[:, B1C + fc:B1C + fc + 1]), 1)
                    hT_ready[fc] = (hs, c)
                    f_ps01_free[fc % 2] = (sACT, c)
                block.scalar(fge)
            f_fill_done[0] = sPE.n

            for l in ([] if _NODRAIN else range(LC)):
                stops = {}

                def fd2(tensor, gi=gi, grp=grp, l=l, stops=stops):
                    if l == 0:
                        if gi == 0:
                            W(tensor, sB2, f_b2)
                        for fc in grp:
                            hs, c = hT_ready[fc]
                            W(tensor, sACT, c)
                            _, s2, c2 = w2q_ready[fc]
                            W(tensor, s2, c2)
                    for dh in range(2):
                        pb = 2 + dh
                        W(tensor, *f_ps23_free[dh])
                        i = None
                        if gi == 0:
                            for db in range(2):
                                tensor.matmul(
                                    PS[pb][:, db * 512:db * 512 + 512], onesrow,
                                    ROWS[0:1, dh * 1024 + db * 512:dh * 1024 + db * 512 + 512],
                                    start=True, stop=False)
                        for k, fc in enumerate(grp):
                            hs, _ = hT_ready[fc]
                            os_, _, _ = w2q_ready[fc]
                            for db in range(2):
                                i = tensor.matmul(
                                    PS[pb][:, db * 512:db * 512 + 512],
                                    AB[:, hs * 1024 + l * 128:hs * 1024 + l * 128 + 128],
                                    OUTT[:, os_ * 2048 + dh * 1024 + db * 512:
                                         os_ * 2048 + dh * 1024 + db * 512 + 512],
                                    start=(gi > 0 and k == 0),
                                    stop=(k == len(grp) - 1))
                        stops[dh] = sPE.inc(i, 1)
                        f_ps23_free[dh] = (sDVE, 0)
                block.tensor(fd2)

                def fsp(vector, gi=gi, l=l, stops=stops):
                    if gi == 0 and l == 0:
                        W(vector, semST[0], e_st_final[0])
                        W(vector, semST[1], e_st_final[1])
                    for dh in range(2):
                        W(vector, sPE, stops[dh])
                        dst = GA[:, l * 2048 + dh * 1024:l * 2048 + dh * 1024 + 1024]
                        if gi == 0:
                            i = vector.tensor_copy(out=dst, in_=PS[2 + dh][:, :])
                        else:
                            i = vector.tensor_tensor(out=dst, in0=dst,
                                                     in1=PS[2 + dh][:, :],
                                                     op=ALU.add)
                        c = sDVE.inc(i, 1)
                        f_ps23_free[dh] = (sDVE, c)
                        spill_dve[(gi, l, dh)] = c
                block.vector(fsp)

            pe_now = sPE.n
            for fc in (grp if not _NODRAIN else []):
                hs, _ = hT_ready[fc]
                os_, _, _ = w2q_ready[fc]
                hT_free[hs] = (sPE, pe_now)
                w2q_free[os_] = (sPE, pe_now)

        gi_last = len(_FG) - 1

        if stop_after == "F":
            return _finish(nc, block, ALLSEMS, tap, locals())

        XS = XnT[:, :].bitcast(F32)
        prev_add = [0]
        prev_out = [0]
        g_sil = {}
        g_ld = {}
        for l in range(LC):
            def fg1(scalar, l=l):
                if l == 0:
                    W(scalar, sPE, f_fill_done[0])
                W(scalar, sSCD, sSCD.n)
                W(scalar, sDVE, prev_add[0])
                sSCD.inc(scalar.dma_start(
                    out=XS[:, 0:2048], in_=d_xtm[:, l * 2048:l * 2048 + 2048]), 16)
                sSCD.inc(scalar.dma_start(
                    out=XS[:, 2048:4096], in_=d_s[l * 128:l * 128 + 128, :]), 16)
                W(scalar, sSCD, sSCD.n)
                W(scalar, sDVE, spill_dve[(gi_last, l, 1)])
                c3 = sACT.inc(scalar.activation(
                    out=XS[:, 4096:6144], in_=GA[:, l * 2048:l * 2048 + 2048],
                    func=AF.Silu), 1)
                g_sil[l] = c3
            block.scalar(fg1)

            def fg2(vector, l=l):
                W(vector, sACT, g_sil[l])
                c = sDVE.inc(vector.tensor_tensor(
                    out=XS[:, 6144:8192], in0=XS[:, 2048:4096],
                    in1=XS[:, 4096:6144], op=ALU.mult), 1)
                W(vector, sDVE, c)
                prev_add[0] = sDVE.inc(vector.tensor_tensor(
                    out=XS[:, 6144:8192], in0=XS[:, 6144:8192],
                    in1=XS[:, 0:2048], op=ALU.add), 1)
            block.vector(fg2)

            def fg3(scalar, l=l):
                W(scalar, sDVE, prev_add[0])
                prev_out[0] = sSCD.inc(scalar.dma_start(
                    out=d_out[:, l * 2048:l * 2048 + 2048],
                    in_=XS[:, 6144:8192]), 16)
            block.scalar(fg3)

        return _finish(nc, block, ALLSEMS, tap, locals())


def _finish(nc, block, allsems, tap, env):
    sDMA, sSCD = env["sDMA"], env["sSCD"]
    sDVE = env["sDVE"]

    if tap is not None:
        d_dbg = env["d_dbg"]
        WST = env["WST"]

        def ftap_pre(sync):
            for s in allsems:
                if s.n > 0:
                    sync.wait_ge(s.h, s.n)
        block.sync(ftap_pre)

        if tap == "s_spill":
            d_s = env["d_s"]

            def ftapd(sync):
                sDMA.inc(sync.dma_start(
                    out=d_dbg.rearrange("p (a d) -> (p a) d", a=8, d=2048),
                    in_=d_s[:, :]), 16)
            block.sync(ftapd)
            src = None
        else:
            src = env[tap]

        if src is None:
            pass
        elif src.dtype == F32:
            def ftapf(sync):
                n = src.shape[1]
                sDMA.inc(sync.dma_start(out=d_dbg[:, 0:n], in_=src[:, :]), 16)
            block.sync(ftapf)
        else:
            n = src.shape[1]
            for k in range(0, n, 2048):
                w = min(2048, n - k)

                def fcv(vector, k=k, w=w):
                    if k == 0:
                        for s in allsems:
                            if s.n > 0 and s is not sDVE:
                                vector.wait_ge(s.h, s.n)
                    else:
                        vector.wait_ge(sDMA.h, sDMA.n)
                    sDVE.inc(vector.tensor_copy(out=WST[:, 0:w],
                                                in_=src[:, k:k + w]), 1)
                block.vector(fcv)

                def fcd(sync, k=k, w=w):
                    sync.wait_ge(sDVE.h, sDVE.n)
                    sDMA.inc(sync.dma_start(out=d_dbg[:, k:k + w],
                                            in_=WST[:, 0:w]), 16)
                block.sync(fcd)

    def ffin(sync):
        for s in allsems:
            if s.n > 0:
                sync.wait_ge(s.h, s.n)
    block.sync(ffin)
    return nc


def _host_layouts(inputs):
    x = np.asarray(inputs["L0_attn_x"])
    wq = np.asarray(inputs["L0_attn_wq"])
    wk = np.asarray(inputs["L0_attn_wk"])
    wv = np.asarray(inputs["L0_attn_wv"])
    wu = np.asarray(inputs["L0_attn_wu"])
    wo = np.asarray(inputs["L0_attn_wo"])
    w1 = np.asarray(inputs["L0_ffn_w1"])
    b1 = np.asarray(inputs["L0_ffn_b1"])
    w2 = np.asarray(inputs["L0_ffn_w2"])
    b2 = np.asarray(inputs["L0_ffn_b2"])

    def rhs_layout(w):
        return np.ascontiguousarray(
            w.reshape(EB, 512, DC, P).transpose(3, 0, 2, 1).reshape(P, EB * DC * 512))

    def lhs_layout(w, nec):
        return np.ascontiguousarray(
            w.reshape(nec, 128, DC, P).transpose(3, 0, 2, 1).reshape(P, nec * DC * 128))

    def w2_layout(w):
        return np.ascontiguousarray(
            w.reshape(D, FC, P).transpose(2, 1, 0).reshape(P, FC * D))

    consts = np.zeros((P, 8), dtype=np.float32)
    consts[:, 0] = LN_EPS
    consts[:, 1] = SRMS_EXPB
    consts[:, 2] = SRMS_GUARD

    w1l = lhs_layout(w1, FC)
    w2r = w2_layout(w2)
    common = dict(
        wk_r=rhs_layout(wk), wv_r=rhs_layout(wv), wo_r=rhs_layout(wo),
        wq_l=lhs_layout(wq, H), wu_l=lhs_layout(wu, H),
        w1_l=w1l, w2_r=w2r,
        b1r=np.ascontiguousarray(b1.reshape(FC, P).T),
        b2row=np.ascontiguousarray(b2.reshape(1, D)),
        consts=consts,
    )

    in_maps = []
    for c in range(NCORE):
        xc = x[0, c * T:(c + 1) * T, :]
        xT = np.ascontiguousarray(
            xc.reshape(T, DC, P).transpose(2, 1, 0).reshape(P, DC * T))
        xtm = np.ascontiguousarray(
            xc.reshape(LC, P, D).transpose(1, 0, 2).reshape(P, LC * D))
        m = dict(common)
        m["xT"] = xT
        m["xtm"] = xtm
        m["w1slice"] = np.ascontiguousarray(w1l[:, c * 16384:(c + 1) * 16384])
        m["w2slice"] = np.ascontiguousarray(w2r[:, c * 16384:(c + 1) * 16384])
        in_maps.append(m)
    return in_maps


def kernel(**inputs):
    in_maps = _host_layouts(inputs)
    nc = build_full()
    res = run_bass_kernel_spmd(nc, in_maps, core_ids=list(range(NCORE)))
    outs = []
    for c in range(NCORE):
        o = np.asarray(res.results[c]["out"])
        outs.append(o.reshape(P, LC, D).transpose(1, 0, 2).reshape(T, D))
    return np.concatenate(outs, axis=0).reshape(1, 8192, D)
